# version 05
# baseline (speedup 1.0000x reference)
import numpy as np
from contextlib import ExitStack

import concourse.bass as bass
import concourse.mybir as mybir
from concourse.bass_utils import run_bass_kernel_spmd

DBG_R = None
DBG = False
F32 = mybir.dt.float32
BF16 = mybir.dt.bfloat16
I32 = mybir.dt.int32
U8 = mybir.dt.uint8
AF = mybir.ActivationFunctionType
ALU = mybir.AluOpType
AX = mybir.AxisListType

D = 1024
T = 4096
NH = 16
DH = 64
NC = D // 128
NTC = T // 512
EPS = 1e-5
NSLOT = 4


ENGINES = ("PE", "ACT", "DVE", "POOL", "SP")
DMA_QUEUES = ("DMA_SP", "DMA_ACT", "DMA_G")
ISSUER = {"DMA_SP": "SP", "DMA_ACT": "ACT", "DMA_G": "POOL"}


class Op:
    __slots__ = ("engine", "emit", "reads", "writes", "idx", "needs_inc", "deps", "tag")

    def __init__(self, engine, emit, reads, writes, idx, tag):
        self.engine = engine
        self.emit = emit
        self.reads = reads
        self.writes = writes
        self.idx = idx
        self.needs_inc = engine in DMA_QUEUES
        self.deps = []
        self.tag = tag


class Prog:
    def __init__(self, nc):
        self.nc = nc
        self.ops = []
        self.last_write = {}
        self.readers = {}
        self.cursor = 0
        self.last_barrier = None
        self.inc_counts = {e: 0 for e in ENGINES + DMA_QUEUES}
        self.inc_at = {}
        self.watermark = {}

    def op(self, engine, emit, reads=(), writes=(), tag=""):
        o = Op(engine, emit, tuple(reads), tuple(writes), len(self.ops), tag)
        deps = {}
        for r in o.reads:
            w = self.last_write.get(r)
            if w is not None:
                deps[w.idx] = w
        for r in o.writes:
            w = self.last_write.get(r)
            if w is not None:
                deps[w.idx] = w
            for rd in self.readers.get(r, ()):
                deps[rd.idx] = rd
        if self.last_barrier is not None:
            deps.setdefault(self.last_barrier.idx, self.last_barrier)
        for d in deps.values():
            if (
                d.engine == o.engine
                and d.engine not in DMA_QUEUES
                and d.engine != "POOL"
            ):
                continue
            if d.idx < self.cursor and not d.needs_inc:
                raise RuntimeError(
                    f"op {o.tag} depends on already-emitted op {d.tag} without inc; "
                    "phases must end with a barrier"
                )
            o.deps.append(d)
        for r in o.writes:
            self.last_write[r] = o
            self.readers[r] = []
        for r in o.reads:
            self.readers.setdefault(r, []).append(o)
        self.ops.append(o)
        return o

    DOUBLE_DMA = True

    def dma(self, queue, out, in_, reads=(), writes=(), tag=""):
        assert out.shape[0] == 128, (tag, out.shape)

        def emit(eng, out=out, in_=in_):
            return eng.dma_start(out, in_)

        if self.DOUBLE_DMA:
            self.op(queue, emit, reads, writes, tag + "_pre")
        return self.op(queue, emit, reads, writes, tag)

    def barrier(self, tag="barrier"):
        lasts = {}
        for o in self.ops:
            lasts[o.engine] = o
        b = self.op("SP", lambda e: e.nop(), tag=tag)
        b.needs_inc = True
        for eng, o in lasts.items():
            if o is not b and not (o.engine == "SP"):
                b.deps.append(o)
        for r in list(self.last_write.keys()):
            self.last_write[r] = b
            self.readers[r] = []
        self.last_barrier = b
        return b

    def finalize_phase(self, block, sems):
        new_ops = self.ops[self.cursor :]
        for o in new_ops:
            best = {}
            for d in o.deps:
                if d.engine not in best or best[d.engine].idx < d.idx:
                    best[d.engine] = d
            o.deps = list(best.values())
            for d in o.deps:
                d.needs_inc = True
        for o in new_ops:
            if o.needs_inc:
                self.inc_counts[o.engine] += 16 if o.engine in DMA_QUEUES else 1
            self.inc_at[o.idx] = self.inc_counts[o.engine]
        by_stream = {}
        for o in new_ops:
            stream = ISSUER.get(o.engine, o.engine)
            by_stream.setdefault(stream, []).append(o)

        def make_body(stream, ops):
            def body(eng):
                for o in ops:
                    for d in sorted(o.deps, key=lambda d: d.idx):
                        val = self.inc_at[d.idx]
                        key = (stream, d.engine)
                        if val <= self.watermark.get(key, 0):
                            continue
                        eng.wait_ge(sems[d.engine], val)
                        self.watermark[key] = val
                    inst = o.emit(eng)
                    if o.needs_inc:
                        if inst is None:
                            raise RuntimeError(f"op {o.tag} needs inc, emitted None")
                        inst.then_inc(
                            sems[o.engine], 16 if o.engine in DMA_QUEUES else 1
                        )

            return body

        attr = {
            "SP": "sync",
            "POOL": "gpsimd",
            "PE": "tensor",
            "ACT": "scalar",
            "DVE": "vector",
        }
        for stream in ("SP", "POOL", "PE", "ACT", "DVE"):
            if stream in by_stream:
                getattr(block, attr[stream])(make_body(stream, by_stream[stream]))
        self.cursor = len(self.ops)

    def stats(self):
        from collections import Counter

        c = Counter(o.engine for o in self.ops)
        ninc = sum(1 for o in self.ops if o.needs_inc)
        ndep = sum(len(o.deps) for o in self.ops)
        return f"ops={len(self.ops)} incs={ninc} deps={ndep} {dict(c)}"




def host_layout(core, I):
    d = {}
    x = np.ascontiguousarray(np.asarray(I["L0_attn_x"][core]))
    d["xT"] = np.ascontiguousarray(x.T)

    wq = np.asarray(I["L0_attn_w_qkv"])
    d["l0_wqkv_h"] = np.ascontiguousarray(
        np.stack(
            [
                np.concatenate(
                    [wq[:, h * 64 : h * 64 + 64],
                     wq[:, D + h * 64 : D + h * 64 + 64],
                     wq[:, 2 * D + h * 64 : 2 * D + h * 64 + 64]],
                    axis=1,
                )
                for h in range(NH)
            ]
        )
    )
    bq = np.asarray(I["L0_attn_b_qkv"])
    bqh = np.stack(
        [
            np.concatenate(
                [bq[h * 64 : h * 64 + 64],
                 bq[D + h * 64 : D + h * 64 + 64],
                 bq[2 * D + h * 64 : 2 * D + h * 64 + 64]]
            )
            for h in range(NH)
        ]
    )
    biasrow = np.concatenate([bqh.reshape(-1), np.asarray(I["L0_attn_b_proj"])])
    d["l0_bias_pad"] = np.ascontiguousarray(
        np.broadcast_to(biasrow[None, :], (128, NH * 192 + D))
    )
    d["l0_wproj"] = np.ascontiguousarray(np.asarray(I["L0_attn_w_proj"]))

    kafull = np.zeros((128, T), np.float32)
    kafull[0] = 1.0
    d["l0_ka_full"] = kafull

    qw = np.asarray(I["L0_ffn_qweights"])
    d["l0f_qT"] = np.ascontiguousarray(np.transpose(qw, (0, 2, 1)))
    sc = np.asarray(I["L0_ffn_scales"])
    d["l0f_sT"] = np.ascontiguousarray(
        np.transpose(np.repeat(sc, 16, axis=2), (0, 2, 1))
    )
    d["l0f_b_pad"] = np.ascontiguousarray(
        np.broadcast_to(np.asarray(I["L0_ffn_biases"])[:, None, :], (6, 128, D))
    )
    la = np.asarray(I["L0_ffn_lora_a"])
    d["l0f_laT"] = np.ascontiguousarray(np.transpose(la, (0, 2, 1)))
    lb = np.asarray(I["L0_ffn_lora_b"])
    lbT = np.transpose(lb, (0, 2, 1))
    d["l0f_lbT_pad"] = np.ascontiguousarray(
        np.concatenate([lbT, np.zeros((6, 96, D), lbT.dtype)], axis=1)
    )
    lg = np.asarray(I["L0_ffn_ln_gamma"])
    lbta = np.asarray(I["L0_ffn_ln_beta"])
    cols = []
    for i in range(3):
        cols.append(lg[i].reshape(NC, 128).T)
        cols.append(lbta[i].reshape(NC, 128).T)
    d["l0f_gb"] = np.ascontiguousarray(np.concatenate(cols, axis=1))

    wv = np.asarray(I["L1_attn_w_v"])
    wwqk = np.asarray(I["L1_attn_w_wqk"])
    wqk = np.asarray(I["L1_attn_w_qk"])
    d["l1_w5_h"] = np.ascontiguousarray(
        np.stack(
            [
                np.concatenate(
                    [
                        wv[:, h * 64 : h * 64 + 64],
                        wwqk[:, h * 64 : h * 64 + 64],
                        wwqk[:, D + h * 64 : D + h * 64 + 64],
                        wqk[:, h * 64 : h * 64 + 64],
                        wqk[:, D + h * 64 : D + h * 64 + 64],
                    ],
                    axis=1,
                )
                for h in range(NH)
            ]
        )
    )
    bv = np.asarray(I["L1_attn_b_v"])
    bwqk = np.asarray(I["L1_attn_b_wqk"])
    bqk = np.asarray(I["L1_attn_b_qk"])
    b5h = np.stack(
        [
            np.concatenate(
                [
                    bv[h * 64 : h * 64 + 64],
                    bwqk[h * 64 : h * 64 + 64],
                    bwqk[D + h * 64 : D + h * 64 + 64],
                    bqk[h * 64 : h * 64 + 64],
                    bqk[D + h * 64 : D + h * 64 + 64],
                ]
            )
            for h in range(NH)
        ]
    )
    d["l1_b5_pad"] = np.ascontiguousarray(
        np.broadcast_to(b5h[:, None, :], (NH, 128, 320))
    )
    d["l1_wout"] = np.ascontiguousarray(np.asarray(I["L1_attn_w_out"]))
    d["l1_bout_pad"] = np.ascontiguousarray(
        np.broadcast_to(np.asarray(I["L1_attn_b_out"])[None, :], (128, D))
    )
    wpe = np.asarray(I["L1_attn_w_pos_embedding"])
    pe = np.asarray(I["L1_attn_pos_embedding"])
    d["l1_wbias"] = np.ascontiguousarray(np.tile(wpe[63 - np.arange(64)], 64)).astype(
        np.float32
    )
    d["l1_gbias"] = np.ascontiguousarray(
        pe[np.clip(T - 1 - np.arange(T), 0, 4094)]
    ).astype(np.float32)
    half = np.arange(T) % 128 >= 64
    ind_a = (~half).astype(np.float32)
    ind_b = half.astype(np.float32)
    qm_a = (-100.0 * ~half).astype(np.float32)
    qm_b = (-100.0 * half).astype(np.float32)
    ones_row = np.ones(T, np.float32)
    zeros_row = np.zeros(T, np.float32)

    def padfull(rows):
        a = np.zeros((128, T), np.float32)
        for i, r in enumerate(rows):
            a[i] = r
        return np.ascontiguousarray(a)

    d["l1_kg_full"] = padfull([ones_row, d["l1_gbias"]])
    d["l1_kw_full"] = padfull([ones_row, d["l1_wbias"], ind_a, ind_b])
    d["l1_qg_full"] = padfull([zeros_row, ones_row])
    d["l1_qw_full"] = padfull([zeros_row, ones_row, qm_b, qm_a])
    d["l1_gb_pad"] = np.ascontiguousarray(
        np.broadcast_to(
            np.stack([d["l1_gbias"], d["l1_wbias"]])[:, None, :], (2, 128, T)
        )
    )
    del d["l1_gbias"], d["l1_wbias"]

    k0u = np.asarray(I["L1_ffn_k0"]).astype(np.uint8)
    k1u = np.asarray(I["L1_ffn_k1"]).astype(np.uint8)
    k2u = np.asarray(I["L1_ffn_k2"]).astype(np.uint8)
    d["l1f_k0t"] = np.ascontiguousarray(
        k0u[:D].reshape(8, 128, 32, 128).transpose(2, 1, 0, 3)
    )
    d["l1f_k0b"] = np.ascontiguousarray(
        k0u[D:].reshape(8, 128, 32, 128).transpose(2, 1, 0, 3)
    )
    d["l1f_k1t"] = np.ascontiguousarray(
        k1u.reshape(32, 128, 32, 128).transpose(2, 1, 0, 3)
    )
    d["l1f_k2t"] = np.ascontiguousarray(
        k2u.reshape(32, 128, 8, 128).transpose(2, 1, 0, 3)
    )
    d["l1f_scol"] = np.ascontiguousarray(
        np.broadcast_to(
            np.stack([
                np.asarray(I["L1_ffn_s0"], np.float32),
                np.asarray(I["L1_ffn_s1"], np.float32),
                np.asarray(I["L1_ffn_s2"], np.float32),
            ])[None, :],
            (128, 3),
        )
    )

    d["c_ident"] = np.eye(128, dtype=np.float32)
    d["c_pool"] = np.ascontiguousarray(
        np.broadcast_to(
            np.array([EPS, 30.0, -30.0, 1.0 / D, 0, 0, 0, 0], np.float32)[None, :],
            (128, 8),
        )
    )
    return d




class Bufs:
    pass


def build_common(nc, es):
    B = Bufs()
    B.XS = es.enter_context(nc.sbuf_tensor("XS", [128, NC * T], BF16))
    B.cpool = es.enter_context(nc.sbuf_tensor("cpool", [128, 8], F32))
    B.identb = es.enter_context(nc.sbuf_tensor("identb", [128, 128], BF16))
    B.ones_b = es.enter_context(nc.sbuf_tensor("ones_b", [1, 512], BF16))
    B.ones_f = es.enter_context(nc.sbuf_tensor("ones_f", [1, 128], F32))
    B.onescol = es.enter_context(nc.sbuf_tensor("onescol", [128, 1], BF16))
    B.rows_f = es.enter_context(nc.sbuf_tensor("rows_f", [1, 2080], F32))
    B.scr = es.enter_context(nc.sbuf_tensor("scr", [128, 512], F32))
    B.sq2 = es.enter_context(nc.sbuf_tensor("sq2", [128, 512], BF16))
    B.pb = [es.enter_context(nc.psum_tensor(f"pb{i}", [128, 512], F32)) for i in range(6)]
    B.pbT = [es.enter_context(nc.psum_tensor(f"pbT{i}", [128, 512], BF16)) for i in range(2)]
    return B


def rowslot(B, i):
    return B.rows_f[0:1, i * 512 : (i + 1) * 512]


def rowcell(B, i):
    return B.rows_f[0:1, 2048 + i : 2049 + i]


def scrslot(B, i, rows=128):
    return B.scr[0:rows, 0:512]


def xs_chunk(B, c, tc):
    return B.XS[:, c * T + tc * 512 : c * T + tc * 512 + 512]


def load_common(P, B, dram):
    P.dma("DMA_SP", B.cpool[:], dram["c_pool"], writes=("cpool",), tag="cpool")
    P.dma("DMA_G", B.identb[:], dram["c_ident"], writes=("identb",), tag="ident")
    P.op("POOL", lambda e: e.memset(B.ones_b[:], 1.0), writes=("ones_b",))
    P.op("POOL", lambda e: e.memset(B.ones_f[:], 1.0), writes=("ones_f",))
    P.op("POOL", lambda e: e.memset(B.onescol[:], 1.0), writes=("onescol",))
    xr = dram["xT"].rearrange("(c p) t -> c p t", p=128)
    for c in range(NC):
        P.dma(
            "DMA_G",
            B.XS[:, c * T : (c + 1) * T],
            xr[c],
            writes=tuple(f"XS{c}_{tc}" for tc in range(NTC)),
            tag=f"loadx{c}",
        )


def emit_stats_mms(P, B, xres, xc, sqres, bankA, bankB, fc, tag):
    P.op("PE", lambda e: e.nop() if True else e.ldweights(B.onescol[:, 0:1]), reads=("onescol",), tag=f"ldst{tag}")
    P.op(
        "PE",
        lambda e: e.matmul(
            B.pb[bankA][0:1, :], B.onescol[:, 0:1], xc, start=(fc == 0), stop=(fc == NC - 1)
        ),
        reads=(xres, "onescol", f"pb{bankA}"),
        writes=(f"pb{bankA}",),
        tag=f"mmsA{tag}",
    )
    P.op(
        "PE",
        lambda e: e.matmul(
            B.pb[bankB][0:1, :], B.onescol[:, 0:1], B.sq2[:], start=(fc == 0), stop=(fc == NC - 1)
        ),
        reads=(sqres, "onescol", f"pb{bankB}"),
        writes=(f"pb{bankB}",),
        tag=f"mmsB{tag}",
    )


def ln_apply(P, B, tc, bankA, bankB, src_res, src_ap, out_res, out_ap, out_scr_slot, tag, bcA=3, bcB=4):
    mu = rowslot(B, 0)
    var = rowslot(B, 1)
    arow = rowslot(B, 2)
    brow = rowslot(B, 3)
    rA, rB = f"pb{bankA}", f"pb{bankB}"
    P.op(
        "DVE",
        lambda e: e.tensor_scalar(mu, B.pb[bankA][0:1, :], 1.0 / D, None, ALU.mult),
        reads=(rA,), writes=("row0",), tag=f"mu{tag}",
    )
    P.op(
        "DVE",
        lambda e: e.tensor_scalar(var, B.pb[bankB][0:1, :], 1.0 / D, None, ALU.mult),
        reads=(rB,), writes=("row1",), tag=f"msq{tag}",
    )
    P.op("ACT", lambda e: e.activation(arow, mu, AF.Square), reads=("row0",), writes=("row2",), tag=f"musq{tag}")
    P.op("DVE", lambda e: e.tensor_sub(var, var, arow), reads=("row1", "row2"), writes=("row1",), tag=f"var{tag}")
    P.op(
        "ACT",
        lambda e: e.activation(var, var, AF.Ln, bias=B.cpool[0:1, 0:1]),
        reads=("row1", "cpool"), writes=("row1",), tag=f"lnv{tag}",
    )
    P.op(
        "ACT",
        lambda e: e.activation(arow, var, AF.Exp, scale=-0.5),
        reads=("row1",), writes=("row2",), tag=f"rstd{tag}",
    )
    P.op("DVE", lambda e: e.tensor_mul(brow, mu, arow), reads=("row0", "row2"), writes=("row3",), tag=f"brow{tag}")
    rbA, rbB = f"pb{bcA}", f"pb{bcB}"
    P.op(
        "PE",
        lambda e: e.matmul(B.pb[bcA][:, :], B.ones_f[0:1, 0:128], arow, start=True, stop=True),
        reads=("row2", "ones_f", rbA), writes=(rbA,), tag=f"bcA{tag}",
    )
    P.op(
        "PE",
        lambda e: e.matmul(B.pb[bcB][:, :], B.ones_f[0:1, 0:128], brow, start=True, stop=True),
        reads=("row3", "ones_f", rbB), writes=(rbB,), tag=f"bcB{tag}",
    )
    scr = scrslot(B, out_scr_slot)
    for c in range(NC):
        sap = src_ap(c)
        oap = out_ap(c)
        P.op(
            "DVE",
            lambda e, sap=sap: e.tensor_mul(scr, sap, B.pb[bcA][:, :]),
            reads=(src_res(c), rbA), writes=("scr0",), tag=f"ap1{tag}_{c}",
        )
        P.op(
            "DVE",
            lambda e, oap=oap: e.tensor_sub(oap, scr, B.pb[bcB][:, :]),
            reads=("scr0", rbB), writes=(out_res(c),), tag=f"ap2{tag}_{c}",
        )




def build_l0_attn(P, B, es, dram):
    nc = P.nc
    global DBG_R
    OT = es.enter_context(nc.sbuf_tensor("OT", [128, NC * T], BF16))
    Qa = es.enter_context(nc.sbuf_tensor("Qa", [128, T], BF16))
    Ka = es.enter_context(nc.sbuf_tensor("Ka", [128, T], BF16))
    Va = es.enter_context(nc.sbuf_tensor("Va", [128, 32 * 65], BF16))
    VF = es.enter_context(nc.sbuf_tensor("VF", [64, 512], BF16))
    WQKV = [es.enter_context(nc.sbuf_tensor(f"WQKV{i}", [128, NC * 192], BF16)) for i in range(NSLOT)]
    WP = es.enter_context(nc.sbuf_tensor("WPL0", [128, NC * 1024], BF16))
    PT = [es.enter_context(nc.sbuf_tensor(f"PT{i}", [128, 512], BF16)) for i in range(3)]
    SQ = es.enter_context(nc.sbuf_tensor("SQL0", [64, 512], BF16))
    BIAS = es.enter_context(nc.sbuf_tensor("BIASL0", [128, NH * 192 + D], BF16))

    P.op("POOL", lambda e: e.memset(OT[:], 0.0), writes=tuple(f"OT{c}" for c in range(NC)))
    P.op("POOL", lambda e: e.memset(Qa[:], 0.0), writes=("Qa_z",) + tuple(f"Qa_q{t}" for t in range(NTC)) + tuple(f"Qa_m{t}" for t in range(NTC)))
    P.dma(
        "DMA_G", Ka[:], dram["l0_ka_full"],
        writes=("Ka_z", "Ka_ones") + tuple(f"Ka_k{t}" for t in range(NTC)),
        tag="kafull",
    )
    P.op(
        "POOL",
        lambda e: e.memset(bass.AP(Va, 64, [[32 * 65, 128], [65, 32], [1, 1]]), 1.0),
        writes=("Va_ones",),
    )
    P.dma(
        "DMA_G", BIAS[:], dram["l0_bias_pad"],
        writes=("bias_qkv", "bias_proj"), tag="biaspad",
    )
    P.dma(
        "DMA_G", WP[:].rearrange("p (c f) -> p c f", f=1024),
        dram["l0_wproj"].rearrange("(c p) f -> p c f", p=128),
        writes=("WPl0",), tag="wproj",
    )

    maxkneg = rowcell(B, 0)
    mk_tmp = rowcell(B, 1)
    normrow = rowslot(B, 0)

    def proj_tensor(h, ws, which, tc, bank, tag):
        pp = B.pb[bank]
        pres = f"pb{bank}"
        off = which * 64
        for kc in range(NC):
            lhsT = WQKV[ws][:, kc * 192 + off : kc * 192 + off + 64]
            P.op("PE", lambda e, l=lhsT: e.nop() if True else e.ldweights(l), reads=(f"WQKVs{ws}",), tag=f"ldw{tag}_{kc}")
            P.op(
                "PE",
                lambda e, l=lhsT, kc=kc: e.matmul(
                    pp[0:64, :], l, xs_chunk(B, kc, tc), start=(kc == 0), stop=False
                ),
                reads=(f"WQKVs{ws}", f"XS{kc}_{tc}", pres),
                writes=(pres,),
                tag=f"mm{tag}_{kc}",
            )
        brow = BIAS[0:1, h * 192 + off : h * 192 + off + 64]
        P.op("PE", lambda e, b=brow: e.nop() if True else e.ldweights(b), reads=("bias_qkv",), tag=f"ldwb{tag}")
        P.op(
            "PE",
            lambda e, b=brow: e.matmul(pp[0:64, :], b, B.ones_b[:], start=False, stop=True),
            reads=("bias_qkv", "ones_b", pres),
            writes=(pres,),
            tag=f"mmb{tag}",
        )
        return pp, pres

    for h in range(NH):
        ws = h % NSLOT
        P.dma(
            "DMA_G",
            WQKV[ws][:].rearrange("p (c f) -> p c f", f=192),
            dram["l0_wqkv_h"].rearrange("h (c p) f -> h p c f", p=128)[h],
            writes=(f"WQKVs{ws}",),
            tag=f"wqkv{h}",
        )
        for tc in range(NTC):
            pp, pres = proj_tensor(h, ws, 1, tc, 5, f"k{h}_{tc}")
            P.op(
                "ACT",
                lambda e, tc=tc, p=pp: e.activation(Ka[64:128, tc * 512 : tc * 512 + 512], p[0:64, :], AF.Copy),
                reads=(pres,), writes=(f"Ka_k{tc}",), tag=f"evK{h}_{tc}",
            )
            P.op(
                "ACT",
                lambda e, p=pp: e.activation(SQ[:], p[0:64, :], AF.Square),
                reads=(pres,), writes=("SQ",), tag=f"sqK{h}_{tc}",
            )
            P.op("PE", lambda e: e.nop() if True else e.ldweights(B.onescol[0:64, 0:1]), reads=("onescol",), tag=f"lnk{h}_{tc}")
            P.op(
                "PE",
                lambda e: e.matmul(B.pb[4][0:1, :], B.onescol[0:64, 0:1], SQ[:], start=True, stop=True),
                reads=("onescol", "SQ", "pb4"), writes=("pb4",), tag=f"mmnk{h}_{tc}",
            )
            P.op(
                "DVE",
                lambda e: e.reduce_max(mk_tmp, B.pb[4][0:1, :], axis=AX.X),
                reads=("pb4",), writes=("cell1",), tag=f"rmax{h}_{tc}",
            )
            if tc == 0:
                P.op("DVE", lambda e: e.tensor_copy(maxkneg, mk_tmp), reads=("cell1",), writes=("cell0",), tag=f"mk0_{h}")
            else:
                P.op(
                    "DVE",
                    lambda e: e.tensor_max(maxkneg, maxkneg, mk_tmp),
                    reads=("cell1", "cell0"), writes=("cell0",), tag=f"mka{h}_{tc}",
                )
        P.op("ACT", lambda e: e.activation(maxkneg, maxkneg, AF.Sqrt), reads=("cell0",), writes=("cell0",), tag=f"mks{h}")
        P.op(
            "DVE",
            lambda e: e.tensor_scalar(maxkneg, maxkneg, -1.0, None, ALU.mult),
            reads=("cell0",), writes=("cell0",), tag=f"mkn{h}",
        )
        for tc in range(NTC):
            pp, pres = proj_tensor(h, ws, 0, tc, 5, f"q{h}_{tc}")
            P.op(
                "ACT",
                lambda e, tc=tc, p=pp: e.activation(
                    Qa[64:128, tc * 512 : tc * 512 + 512], p[0:64, :], AF.Copy, scale=0.125
                ),
                reads=(pres,), writes=(f"Qa_q{tc}",), tag=f"evQ{h}_{tc}",
            )
            P.op(
                "ACT",
                lambda e, p=pp: e.activation(SQ[:], p[0:64, :], AF.Square, scale=0.125),
                reads=(pres,), writes=("SQ",), tag=f"sqQ{h}_{tc}",
            )
            P.op("PE", lambda e: e.nop() if True else e.ldweights(B.onescol[0:64, 0:1]), reads=("onescol",), tag=f"lnq{h}_{tc}")
            P.op(
                "PE",
                lambda e: e.matmul(B.pb[4][0:1, :], B.onescol[0:64, 0:1], SQ[:], start=True, stop=True),
                reads=("onescol", "SQ", "pb4"), writes=("pb4",), tag=f"mmnq{h}_{tc}",
            )
            P.op(
                "ACT",
                lambda e: e.activation(normrow, B.pb[4][0:1, :], AF.Sqrt),
                reads=("pb4",), writes=("row0",), tag=f"nq{h}_{tc}",
            )
            P.op(
                "DVE",
                lambda e, tc=tc: e.tensor_scalar(
                    Qa[0:1, tc * 512 : tc * 512 + 512], normrow, maxkneg, 30.0, ALU.mult, ALU.add
                ),
                reads=("row0", "cell0"), writes=(f"Qa_m{tc}",), tag=f"mh{h}_{tc}",
            )
        for tc in range(NTC):
            pp, pres = proj_tensor(h, ws, 2, tc, 5, f"v{h}_{tc}")
            P.op(
                "ACT",
                lambda e, p=pp: e.activation(VF[0:64, :], p[0:64, :], AF.Copy),
                reads=(pres,), writes=("VF",), tag=f"evV{h}_{tc}",
            )
            for j in range(4):
                jt = tc * 4 + j
                ps = jt % 2
                P.op(
                    "PE",
                    lambda e, j=j, ps=ps: e.transpose(
                        B.pbT[ps][:, 0:64],
                        VF[0:64, j * 128 : j * 128 + 128],
                        B.identb[0:64, 0:64],
                    ),
                    reads=("VF", "identb", f"pbT{ps}"), writes=(f"pbT{ps}",), tag=f"trV{h}_{jt}",
                )
                P.op(
                    "DVE",
                    lambda e, jt=jt, ps=ps: e.tensor_copy(
                        Va[:, jt * 65 : jt * 65 + 64], B.pbT[ps][:, 0:64]
                    ),
                    reads=(f"pbT{ps}",), writes=(f"Va{jt}",), tag=f"evT{h}_{jt}",
                )
        for si in range(NTC):
            icols = slice(si * 512, si * 512 + 512)
            for jt in range(32):
                bk = jt % 3
                bank = B.pb[bk]
                bres = f"pb{bk}"
                lhsT = Ka[:, jt * 128 : jt * 128 + 128]
                P.op(
                    "PE",
                    lambda e, l=lhsT: e.nop() if True else e.ldweights(l),
                    reads=("Ka_ones", f"Ka_k{jt // 4}"), tag=f"ldS{h}_{si}_{jt}",
                )
                P.op(
                    "PE",
                    lambda e, l=lhsT, b=bank, ic=icols: e.matmul(b[:, :], l, Qa[:, ic], start=True, stop=True),
                    reads=("Ka_ones", f"Ka_k{jt // 4}", f"Qa_m{si}", f"Qa_q{si}", "Qa_z", bres),
                    writes=(bres,), tag=f"mmS{h}_{si}_{jt}",
                )
                P.op(
                    "ACT",
                    lambda e, b=bank, pt=PT[bk]: e.activation(pt[:], b[:, :], AF.Exp),
                    reads=(bres, f"PT{bk}"), writes=(f"PT{bk}",), tag=f"exp{h}_{si}_{jt}",
                )
                P.op(
                    "PE",
                    lambda e, jt=jt: e.nop() if True else e.ldweights(Va[:, jt * 65 : jt * 65 + 65]),
                    reads=(f"Va{jt}", "Va_ones"), tag=f"ldV{h}_{si}_{jt}",
                )
                P.op(
                    "PE",
                    lambda e, jt=jt, pt=PT[bk]: e.matmul(
                        B.pb[3][0:65, :], Va[:, jt * 65 : jt * 65 + 65], pt[:],
                        start=(jt == 0), stop=(jt == 31),
                    ),
                    reads=(f"Va{jt}", "Va_ones", f"PT{bk}", "pb3"),
                    writes=("pb3",), tag=f"mmPV{h}_{si}_{jt}",
                )
            rl = rowslot(B, 1)
            P.op(
                "DVE",
                lambda e: e.reciprocal(rl, B.pb[3][64:65, :]),
                reads=("pb3",), writes=("row1",), tag=f"rec{h}_{si}",
            )
            P.op(
                "PE",
                lambda e: e.matmul(B.pb[4][0:64, :], B.ones_f[0:1, 0:64], rl, start=True, stop=True),
                reads=("row1", "ones_f", "pb4"), writes=("pb4",), tag=f"bcl{h}_{si}",
            )
            scr = scrslot(B, 0)
            P.op(
                "DVE",
                lambda e: e.tensor_copy(scr[0:64, :], B.pb[3][0:64, :]),
                reads=("pb3",), writes=("scr0",), tag=f"oev{h}_{si}",
            )
            osl = OT[(h % 2) * 64 : (h % 2) * 64 + 64,
                     (h // 2) * T + si * 512 : (h // 2) * T + si * 512 + 512]
            P.op(
                "DVE",
                lambda e, o=osl: e.tensor_mul(o, scr[0:64, :], B.pb[4][0:64, :]),
                reads=("scr0", "pb4"), writes=(f"OT{h // 2}",), tag=f"onr{h}_{si}",
            )

    for tc in range(NTC):
        for fc in range(NC):
            bank = 5 if fc % 2 == 0 else 2
            pp = B.pb[bank]
            pres = f"pb{bank}"
            for kc in range(NC):
                lhsT = WP[:, kc * 1024 + fc * 128 : kc * 1024 + fc * 128 + 128]
                P.op("PE", lambda e, l=lhsT: e.nop() if True else e.ldweights(l), reads=("WPl0",), tag=f"ldp{tc}_{fc}_{kc}")
                P.op(
                    "PE",
                    lambda e, l=lhsT, kc=kc, p=pp, tc=tc: e.matmul(
                        p[:, :], l, OT[:, kc * T + tc * 512 : kc * T + tc * 512 + 512],
                        start=(kc == 0), stop=False,
                    ),
                    reads=("WPl0", f"OT{kc}", pres), writes=(pres,), tag=f"mmp{tc}_{fc}_{kc}",
                )
            brow = BIAS[0:1, NH * 192 + fc * 128 : NH * 192 + fc * 128 + 128]
            P.op("PE", lambda e, b=brow: e.nop() if True else e.ldweights(b), reads=("bias_proj",), tag=f"ldbp{tc}_{fc}")
            P.op(
                "PE",
                lambda e, b=brow, p=pp: e.matmul(p[:, :], b, B.ones_b[:], start=False, stop=True),
                reads=("bias_proj", "ones_b", pres), writes=(pres,), tag=f"mmbp{tc}_{fc}",
            )
            xc = xs_chunk(B, fc, tc)
            P.op(
                "DVE",
                lambda e, x=xc, p=pp: e.tensor_add(x, p[:, :], x),
                reads=(pres, f"XS{fc}_{tc}"), writes=(f"XS{fc}_{tc}",), tag=f"res{tc}_{fc}",
            )
            P.op(
                "ACT",
                lambda e, x=xc: e.activation(B.sq2[:], x, AF.Square),
                reads=(f"XS{fc}_{tc}",), writes=("sq2",), tag=f"rsq{tc}_{fc}",
            )
            if DBG_R is not None:
                P.dma(
                    "DMA_SP",
                    DBG_R[0].rearrange("(c p) t -> c p t", p=128)[fc, :, tc * 512 : tc * 512 + 512],
                    xc,
                    reads=(f"XS{fc}_{tc}",), writes=(f"rdbg{fc}_{tc}",), tag=f"rdbg{tc}_{fc}",
                )
            emit_stats_mms(P, B, f"XS{fc}_{tc}", xc, "sq2", 0, 1, fc, f"l0a{tc}_{fc}")
        ln_apply(
            P, B, tc, 0, 1,
            lambda c, tc=tc: f"XS{c}_{tc}",
            lambda c, tc=tc: xs_chunk(B, c, tc),
            lambda c, tc=tc: f"XS{c}_{tc}",
            lambda c, tc=tc: xs_chunk(B, c, tc),
            1, f"l0a{tc}",
        )




def build_l1_attn(P, B, es, dram):
    nc = P.nc
    OT = es.enter_context(nc.sbuf_tensor("OT1", [128, NC * T], BF16))
    Qg = es.enter_context(nc.sbuf_tensor("Qg", [128, T], BF16))
    Kg = es.enter_context(nc.sbuf_tensor("Kg", [128, T], BF16))
    Qw = es.enter_context(nc.sbuf_tensor("Qw", [128, T], BF16))
    Kw = es.enter_context(nc.sbuf_tensor("Kw", [128, T], BF16))
    Va = es.enter_context(nc.sbuf_tensor("Va1", [128, 32 * 65], BF16))

    W5 = [es.enter_context(nc.sbuf_tensor(f"W5_{i}", [128, NC * 320], BF16)) for i in range(2)]
    WOS = [es.enter_context(nc.sbuf_tensor(f"WOS{i}", [128, NC * 128], BF16)) for i in range(2)]
    PT = [es.enter_context(nc.sbuf_tensor(f"PT1{i}", [128, 512], BF16)) for i in range(3)]
    PTW = es.enter_context(nc.sbuf_tensor("PTW", [128, 128], BF16))
    SQ = es.enter_context(nc.sbuf_tensor("SQL1", [64, 512], BF16))
    VF = SQ
    B5 = [es.enter_context(nc.sbuf_tensor(f"B5_{i}", [128, 320], BF16)) for i in range(2)]
    BOUT = es.enter_context(nc.sbuf_tensor("BOUT", [128, D], BF16))

    P.op("POOL", lambda e: e.memset(OT[:], 0.0), writes=tuple(f"OT{c}" for c in range(NC)))
    P.dma("DMA_G", Qg[:], dram["l1_qg_full"],
          writes=("Qg_z", "Qg_ones") + tuple(f"Qg_q{t}" for t in range(NTC)) + tuple(f"Qg_m{t}" for t in range(NTC)),
          tag="qgfull")
    P.dma("DMA_G", Kg[:], dram["l1_kg_full"],
          writes=("Kg_z", "Kg_ones", "Kg_g") + tuple(f"Kg_k{t}" for t in range(NTC)), tag="kgfull")
    P.dma("DMA_G", Qw[:], dram["l1_qw_full"],
          writes=("Qw_z", "Qw_ones", "Qw_mb", "Qw_ma") + tuple(f"Qw_q{t}" for t in range(NTC)) + tuple(f"Qw_m{t}" for t in range(NTC)),
          tag="qwfull")
    P.dma("DMA_G", Kw[:], dram["l1_kw_full"],
          writes=("Kw_z", "Kw_ones", "Kw_wb", "Kw_ia", "Kw_ib") + tuple(f"Kw_k{t}" for t in range(NTC)), tag="kwfull")

    P.op(
        "POOL",
        lambda e: e.memset(bass.AP(Va, 64, [[32 * 65, 128], [65, 32], [1, 1]]), 1.0),
        writes=("Va_ones",),
    )
    P.dma("DMA_G", BOUT[:], dram["l1_bout_pad"], writes=("bias_out",), tag="bout")

    c30mg = rowcell(B, 2)
    c30mw = rowcell(B, 3)
    gtmp = rowcell(B, 5)
    growslot = rowslot(B, 3)
    for name, cell, cres, dsrc in (
        ("g", c30mg, "cell2", "l1_gbias"),
        ("w", c30mw, "cell3", "l1_wbias"),
    ):
        gi = 0 if dsrc == "l1_gbias" else 1
        for ch in range(NTC):
            P.dma(
                "DMA_SP", scrslot(B, 0), dram["l1_gb_pad"][gi, :, ch * 512 : ch * 512 + 512],
                reads=("scr0",), writes=("scr0",), tag=f"gld{name}{ch}",
            )
            P.op("DVE", lambda e: e.reduce_max(gtmp, scrslot(B, 0)[0:1, :], axis=AX.X), reads=("scr0",), writes=("cell5",), tag=f"grm{name}{ch}")
            if ch == 0:
                P.op("DVE", lambda e, c=cell: e.tensor_copy(c, gtmp), reads=("cell5",), writes=(cres,), tag=f"gc{name}{ch}")
            else:
                P.op("DVE", lambda e, c=cell: e.tensor_max(c, c, gtmp), reads=("cell5", cres), writes=(cres,), tag=f"gm{name}{ch}")
        P.op("DVE", lambda e, c=cell: e.tensor_scalar(c, c, -1.0, 30.0, ALU.mult, ALU.add), reads=(cres,), writes=(cres,), tag=f"c30m{name}")

    mkg = rowcell(B, 0)
    mkw = rowcell(B, 1)
    mtmp = rowcell(B, 4)
    normrow = rowslot(B, 0)

    def proj5(h, ws, idx, tc, bank, tag):
        pp = B.pb[bank]
        pres = f"pb{bank}"
        off = idx * 64
        for kc in range(NC):
            lhsT = W5[ws][:, kc * 320 + off : kc * 320 + off + 64]
            P.op("PE", lambda e, l=lhsT: e.nop() if True else e.ldweights(l), reads=(f"W5s{ws}",), tag=f"ldw{tag}_{kc}")
            P.op(
                "PE",
                lambda e, l=lhsT, kc=kc, p=pp: e.matmul(
                    p[0:64, :], l, xs_chunk(B, kc, tc), start=(kc == 0), stop=False
                ),
                reads=(f"W5s{ws}", f"XS{kc}_{tc}", pres), writes=(pres,), tag=f"mm{tag}_{kc}",
            )
        brow = B5[ws][0:1, off : off + 64]
        P.op("PE", lambda e, b=brow: e.nop() if True else e.ldweights(b), reads=(f"B5s{ws}",), tag=f"ldb{tag}")
        P.op(
            "PE",
            lambda e, b=brow, p=pp: e.matmul(p[0:64, :], b, B.ones_b[:], start=False, stop=True),
            reads=(f"B5s{ws}", "ones_b", pres), writes=(pres,), tag=f"mmb{tag}",
        )
        return pp, pres

    def norm_path(pp, pres, scale, negcell, negres, c30, c30res, dstrow, dstres, tag):
        P.op(
            "ACT",
            lambda e: e.activation(SQ[:], pp[0:64, :], AF.Square, scale=scale),
            reads=(pres,), writes=("SQL1",), tag=f"sq{tag}",
        )
        P.op("PE", lambda e: e.nop() if True else e.ldweights(B.onescol[0:64, 0:1]), reads=("onescol",), tag=f"ldn{tag}")
        P.op(
            "PE",
            lambda e: e.matmul(B.pb[4][0:1, :], B.onescol[0:64, 0:1], SQ[:], start=True, stop=True),
            reads=("onescol", "SQL1", "pb4"), writes=("pb4",), tag=f"mmn{tag}",
        )
        P.op(
            "ACT",
            lambda e: e.activation(normrow, B.pb[4][0:1, :], AF.Sqrt),
            reads=("pb4",), writes=("row0",), tag=f"nr{tag}",
        )
        P.op(
            "DVE",
            lambda e, d=dstrow: e.tensor_scalar(d, normrow, negcell, c30, ALU.mult, ALU.add),
            reads=("row0", negres, c30res), writes=(dstres,), tag=f"mh{tag}",
        )

    def knorm_max(pp, pres, scale, negcell, negres, first, tag):
        P.op(
            "ACT",
            lambda e: e.activation(SQ[:], pp[0:64, :], AF.Square, scale=scale),
            reads=(pres,), writes=("SQL1",), tag=f"sq{tag}",
        )
        P.op("PE", lambda e: e.nop() if True else e.ldweights(B.onescol[0:64, 0:1]), reads=("onescol",), tag=f"ldn{tag}")
        P.op(
            "PE",
            lambda e: e.matmul(B.pb[4][0:1, :], B.onescol[0:64, 0:1], SQ[:], start=True, stop=True),
            reads=("onescol", "SQL1", "pb4"), writes=("pb4",), tag=f"mmn{tag}",
        )
        P.op("DVE", lambda e: e.reduce_max(mtmp, B.pb[4][0:1, :], axis=AX.X), reads=("pb4",), writes=("cell4",), tag=f"rm{tag}")
        if first:
            P.op("DVE", lambda e, n=negcell: e.tensor_copy(n, mtmp), reads=("cell4",), writes=(negres,), tag=f"mk0{tag}")
        else:
            P.op("DVE", lambda e, n=negcell: e.tensor_max(n, n, mtmp), reads=("cell4", negres), writes=(negres,), tag=f"mka{tag}")

    for h in range(NH):
        ws = h % 2
        P.dma(
            "DMA_G",
            W5[ws][:].rearrange("p (c f) -> p c f", f=320),
            dram["l1_w5_h"].rearrange("h (c p) f -> h p c f", p=128)[h],
            writes=(f"W5s{ws}",), tag=f"w5{h}",
        )
        P.dma("DMA_G", B5[ws][:], dram["l1_b5_pad"][h], writes=(f"B5s{ws}",), tag=f"b5{h}")
        for tc in range(NTC):
            pp, pres = proj5(h, ws, 4, tc, 5, f"kg{h}_{tc}")
            P.op(
                "ACT",
                lambda e, tc=tc, p=pp: e.activation(Kg[64:128, tc * 512 : tc * 512 + 512], p[0:64, :], AF.Copy),
                reads=(pres,), writes=(f"Kg_k{tc}",), tag=f"evKg{h}_{tc}",
            )
            knorm_max(pp, pres, 1.0, mkg, "cell0", tc == 0, f"kg{h}_{tc}")
        for tc in range(NTC):
            pp, pres = proj5(h, ws, 2, tc, 5, f"kw{h}_{tc}")
            P.op(
                "ACT",
                lambda e, tc=tc, p=pp: e.activation(Kw[64:128, tc * 512 : tc * 512 + 512], p[0:64, :], AF.Copy),
                reads=(pres,), writes=(f"Kw_k{tc}",), tag=f"evKw{h}_{tc}",
            )
            knorm_max(pp, pres, 1.0, mkw, "cell1", tc == 0, f"kw{h}_{tc}")
        for cell, cres, tg in ((mkg, "cell0", f"g{h}"), (mkw, "cell1", f"w{h}")):
            P.op("ACT", lambda e, c=cell: e.activation(c, c, AF.Sqrt), reads=(cres,), writes=(cres,), tag=f"mks{tg}")
            P.op(
                "DVE",
                lambda e, c=cell: e.tensor_scalar(c, c, -1.0, None, ALU.mult),
                reads=(cres,), writes=(cres,), tag=f"mkn{tg}",
            )
        for tc in range(NTC):
            pp, pres = proj5(h, ws, 3, tc, 5, f"qg{h}_{tc}")
            P.op(
                "ACT",
                lambda e, tc=tc, p=pp: e.activation(Qg[64:128, tc * 512 : tc * 512 + 512], p[0:64, :], AF.Copy),
                reads=(pres,), writes=(f"Qg_q{tc}",), tag=f"evQg{h}_{tc}",
            )
            norm_path(
                pp, pres, 1.0, mkg, "cell0", c30mg, "cell2",
                Qg[0:1, tc * 512 : tc * 512 + 512], f"Qg_m{tc}", f"qg{h}_{tc}",
            )
        for tc in range(NTC):
            pp, pres = proj5(h, ws, 1, tc, 5, f"qw{h}_{tc}")
            P.op(
                "ACT",
                lambda e, tc=tc, p=pp: e.activation(Qw[64:128, tc * 512 : tc * 512 + 512], p[0:64, :], AF.Copy, scale=0.125),
                reads=(pres,), writes=(f"Qw_q{tc}",), tag=f"evQw{h}_{tc}",
            )
            norm_path(
                pp, pres, 0.125, mkw, "cell1", c30mw, "cell3",
                Qw[0:1, tc * 512 : tc * 512 + 512], f"Qw_m{tc}", f"qw{h}_{tc}",
            )
        for tc in range(NTC):
            pp, pres = proj5(h, ws, 0, tc, 5, f"v{h}_{tc}")
            P.op(
                "ACT",
                lambda e, p=pp: e.activation(VF[0:64, :], p[0:64, :], AF.Copy),
                reads=(pres,), writes=("SQL1",), tag=f"evV{h}_{tc}",
            )
            for j in range(4):
                jt = tc * 4 + j
                ps = jt % 2
                P.op(
                    "PE",
                    lambda e, j=j, ps=ps: e.transpose(
                        B.pbT[ps][:, 0:64], VF[0:64, j * 128 : j * 128 + 128],
                        B.identb[0:64, 0:64],
                    ),
                    reads=("SQL1", "identb", f"pbT{ps}"), writes=(f"pbT{ps}",), tag=f"trV1{h}_{jt}",
                )
                P.op(
                    "DVE",
                    lambda e, jt=jt, ps=ps: e.tensor_copy(
                        Va[:, jt * 65 : jt * 65 + 64], B.pbT[ps][:, 0:64]
                    ),
                    reads=(f"pbT{ps}",), writes=(f"Va{jt}",), tag=f"evT1{h}_{jt}",
                )
        for si in range(NTC):
            icols = slice(si * 512, si * 512 + 512)
            for jt in range(32):
                bk = jt % 3
                bank = B.pb[bk]
                bres = f"pb{bk}"
                lhsT = Kg[:, jt * 128 : jt * 128 + 128]
                P.op(
                    "PE",
                    lambda e, l=lhsT: e.nop() if True else e.ldweights(l),
                    reads=("Kg_ones", "Kg_g", f"Kg_k{jt // 4}"), tag=f"ldS1{h}_{si}_{jt}",
                )
                P.op(
                    "PE",
                    lambda e, l=lhsT, b=bank, ic=icols: e.matmul(b[:, :], l, Qg[:, ic], start=True, stop=True),
                    reads=("Kg_ones", "Kg_g", f"Kg_k{jt // 4}", f"Qg_m{si}", f"Qg_q{si}", "Qg_z", "Qg_ones", bres),
                    writes=(bres,), tag=f"mmS1{h}_{si}_{jt}",
                )
                P.op(
                    "ACT",
                    lambda e, b=bank, pt=PT[bk]: e.activation(pt[:], b[:, :], AF.Exp),
                    reads=(bres, f"PT1{bk}"), writes=(f"PT1{bk}",), tag=f"exp1{h}_{si}_{jt}",
                )
                P.op(
                    "PE",
                    lambda e, jt=jt: e.nop() if True else e.ldweights(Va[:, jt * 65 : jt * 65 + 65]),
                    reads=(f"Va{jt}", "Va_ones"), tag=f"ldV1{h}_{si}_{jt}",
                )
                P.op(
                    "PE",
                    lambda e, jt=jt, pt=PT[bk]: e.matmul(
                        B.pb[3][0:65, :], Va[:, jt * 65 : jt * 65 + 65], pt[:],
                        start=(jt == 0), stop=(jt == 31),
                    ),
                    reads=(f"Va{jt}", "Va_ones", f"PT1{bk}", "pb3"),
                    writes=("pb3",), tag=f"mmPV1{h}_{si}_{jt}",
                )
            for j in range(4):
                jt = si * 4 + j
                lhsT = Kw[:, jt * 128 : jt * 128 + 128]
                qcols = slice(jt * 128, jt * 128 + 128)
                P.op(
                    "PE",
                    lambda e, l=lhsT: e.nop() if True else e.ldweights(l),
                    reads=("Kw_ones", "Kw_wb", "Kw_ia", "Kw_ib", f"Kw_k{jt // 4}"),
                    tag=f"ldSw{h}_{si}_{j}",
                )
                P.op(
                    "PE",
                    lambda e, l=lhsT, qc=qcols: e.matmul(B.pb[2][:, 0:128], l, Qw[:, qc], start=True, stop=True),
                    reads=("Kw_ones", "Kw_wb", "Kw_ia", "Kw_ib", f"Kw_k{jt // 4}",
                           f"Qw_m{si}", f"Qw_q{si}", "Qw_z", "Qw_ones", "Qw_mb", "Qw_ma", "pb2"),
                    writes=("pb2",), tag=f"mmSw{h}_{si}_{j}",
                )
                P.op(
                    "ACT",
                    lambda e: e.activation(PTW[:], B.pb[2][:, 0:128], AF.Exp),
                    reads=("pb2", "PTW"), writes=("PTW",), tag=f"expw{h}_{si}_{j}",
                )
                P.op(
                    "PE",
                    lambda e, jt=jt: e.nop() if True else e.ldweights(Va[:, jt * 65 : jt * 65 + 65]),
                    reads=(f"Va{jt}", "Va_ones"), tag=f"ldVw{h}_{si}_{j}",
                )
                P.op(
                    "PE",
                    lambda e, jt=jt, j=j: e.matmul(
                        B.pb[5][0:65, j * 128 : j * 128 + 128],
                        Va[:, jt * 65 : jt * 65 + 65], PTW[:],
                        start=True, stop=True,
                    ),
                    reads=(f"Va{jt}", "Va_ones", "PTW", "pb5"),
                    writes=("pb5",), tag=f"mmPVw{h}_{si}_{j}",
                )
            rl = rowslot(B, 1)
            P.op("DVE", lambda e: e.reciprocal(rl, B.pb[3][64:65, :]), reads=("pb3",), writes=("row1",), tag=f"rec1{h}_{si}")
            P.op(
                "PE",
                lambda e: e.matmul(B.pb[4][0:64, :], B.ones_f[0:1, 0:64], rl, start=True, stop=True),
                reads=("row1", "ones_f", "pb4"), writes=("pb4",), tag=f"bcl1{h}_{si}",
            )
            scr = scrslot(B, 0)
            P.op("DVE", lambda e: e.tensor_copy(scr[0:64, :], B.pb[3][0:64, :]), reads=("pb3",), writes=("scr0",), tag=f"oev1{h}_{si}")
            osl = OT[(h % 2) * 64 : (h % 2) * 64 + 64,
                     (h // 2) * T + si * 512 : (h // 2) * T + si * 512 + 512]
            P.op(
                "DVE",
                lambda e, o=osl: e.tensor_mul(o, scr[0:64, :], B.pb[4][0:64, :]),
                reads=("scr0", "pb4"), writes=(f"OT{h // 2}",), tag=f"onr1{h}_{si}",
            )
            rlw = rowslot(B, 2)
            P.op("DVE", lambda e: e.reciprocal(rlw, B.pb[5][64:65, :]), reads=("pb5",), writes=("row2",), tag=f"recw{h}_{si}")
            P.op(
                "PE",
                lambda e: e.matmul(B.pb[4][0:64, :], B.ones_f[0:1, 0:64], rlw, start=True, stop=True),
                reads=("row2", "ones_f", "pb4"), writes=("pb4",), tag=f"bclw{h}_{si}",
            )
            b0 = (h % 2) * 64
            scr1 = scrslot(B, 1)
            P.op(
                "DVE",
                lambda e, b0=b0: e.tensor_copy(scr1[b0 : b0 + 64, :], B.pb[5][0:64, :]),
                reads=("pb5",), writes=("scr0",), tag=f"oevw{h}_{si}",
            )
            P.op(
                "DVE",
                lambda e, b0=b0: e.tensor_mul(
                    scr1[b0 : b0 + 64, :], scr1[b0 : b0 + 64, :], B.pb[4][0:64, :]
                ),
                reads=("scr0", "pb4"), writes=("scr0",), tag=f"onrw{h}_{si}",
            )
            P.op(
                "DVE",
                lambda e, o=osl, b0=b0: e.tensor_add(o, scr1[b0 : b0 + 64, :], o),
                reads=("scr0", f"OT{h // 2}"), writes=(f"OT{h // 2}",), tag=f"oadd{h}_{si}",
            )

    wo_r = dram["l1_wout"].rearrange("(c p) f -> p c f", p=128)
    for tc in range(NTC):
        for fc in range(NC):
            bank = 5 if fc % 2 == 0 else 2
            pp = B.pb[bank]
            pres = f"pb{bank}"
            wsl = fc % 2
            P.dma(
                "DMA_G",
                WOS[wsl][:].rearrange("p (c f) -> p c f", f=128),
                wo_r[:, :, fc * 128 : fc * 128 + 128],
                writes=(f"WOS{wsl}",), tag=f"wos{tc}_{fc}",
            )
            for kc in range(NC):
                lhsT = WOS[wsl][:, kc * 128 : kc * 128 + 128]
                P.op("PE", lambda e, l=lhsT: e.nop() if True else e.ldweights(l), reads=(f"WOS{wsl}",), tag=f"ldp1{tc}_{fc}_{kc}")
                P.op(
                    "PE",
                    lambda e, l=lhsT, kc=kc, p=pp, tc=tc: e.matmul(
                        p[:, :], l, OT[:, kc * T + tc * 512 : kc * T + tc * 512 + 512],
                        start=(kc == 0), stop=False,
                    ),
                    reads=(f"WOS{wsl}", f"OT{kc}", pres), writes=(pres,), tag=f"mmp1{tc}_{fc}_{kc}",
                )
            brow = BOUT[0:1, fc * 128 : fc * 128 + 128]
            P.op("PE", lambda e, b=brow: e.nop() if True else e.ldweights(b), reads=("bias_out",), tag=f"ldbp1{tc}_{fc}")
            P.op(
                "PE",
                lambda e, b=brow, p=pp: e.matmul(p[:, :], b, B.ones_b[:], start=False, stop=True),
                reads=("bias_out", "ones_b", pres), writes=(pres,), tag=f"mmbp1{tc}_{fc}",
            )
            xc = xs_chunk(B, fc, tc)
            P.op(
                "DVE",
                lambda e, x=xc, p=pp: e.tensor_add(x, p[:, :], x),
                reads=(pres, f"XS{fc}_{tc}"), writes=(f"XS{fc}_{tc}",), tag=f"res1{tc}_{fc}",
            )
            P.op(
                "ACT",
                lambda e, x=xc: e.activation(B.sq2[:], x, AF.Square),
                reads=(f"XS{fc}_{tc}",), writes=("sq2",), tag=f"rsq1{tc}_{fc}",
            )
            emit_stats_mms(P, B, f"XS{fc}_{tc}", xc, "sq2", 0, 1, fc, f"l1a{tc}_{fc}")
        ln_apply(
            P, B, tc, 0, 1,
            lambda c, tc=tc: f"XS{c}_{tc}",
            lambda c, tc=tc: xs_chunk(B, c, tc),
            lambda c, tc=tc: f"XS{c}_{tc}",
            lambda c, tc=tc: xs_chunk(B, c, tc),
            1, f"l1a{tc}",
        )




def build_l0_ffn(P, B, es, dram):
    nc = P.nc
    HB = es.enter_context(nc.sbuf_tensor("HB", [128, NC * T], BF16))
    Y1 = es.enter_context(nc.sbuf_tensor("Y1", [128, NC * 512], BF16))
    Z = es.enter_context(nc.sbuf_tensor("Z", [128, NC * 512], BF16))
    W = [es.enter_context(nc.sbuf_tensor(f"WF{i}", [128, NC * D], BF16)) for i in range(2)]
    LA = [es.enter_context(nc.sbuf_tensor(f"LAF{i}", [128, NC * 32], BF16)) for i in range(2)]
    LB = [es.enter_context(nc.sbuf_tensor(f"LBF{i}", [128, D], BF16)) for i in range(2)]
    U1 = es.enter_context(nc.sbuf_tensor("U1", [32, 512], BF16))
    BIASF = [es.enter_context(nc.sbuf_tensor(f"BIASF{i}", [128, D], BF16)) for i in range(2)]
    GB = es.enter_context(nc.sbuf_tensor("GBF", [128, 48], F32))
    QTF = [es.enter_context(nc.sbuf_tensor(f"QTF{i}", [128, 512], I32)) for i in range(1)]
    SCT = [es.enter_context(nc.sbuf_tensor(f"SCT{i}", [128, 512], F32)) for i in range(1)]

    P.dma("DMA_SP", GB[:], dram["l0f_gb"], writes=("GB",), tag="gb")

    def dequant(lin, tag):
        ws = lin % 2
        wres = f"WF{ws}"
        for kc in range(NC):
            for oh in range(2):
                sl = 0
                qt = QTF[sl]
                st = SCT[sl]
                P.dma(
                    "DMA_G",
                    qt[:],
                    dram["l0f_qT"].rearrange("l (c p) o -> l c p o", p=128)[
                        lin, kc, :, oh * 512 : oh * 512 + 512
                    ],
                    reads=(), writes=(f"QTF{sl}",), tag=f"qT{tag}_{kc}_{oh}",
                )
                P.dma(
                    "DMA_SP",
                    st[:],
                    dram["l0f_sT"].rearrange("l (c p) o -> l c p o", p=128)[
                        lin, kc, :, oh * 512 : oh * 512 + 512
                    ],
                    reads=(), writes=(f"SCT{sl}",), tag=f"sT{tag}_{kc}_{oh}",
                )
                scrq = scrslot(B, 0)
                P.op(
                    "DVE",
                    lambda e, qt=qt: e.tensor_scalar(scrq, qt[:], 1.0 / 7.5, -1.0, ALU.mult, ALU.add),
                    reads=(f"QTF{sl}",), writes=("scr0",), tag=f"dqa{tag}_{kc}_{oh}",
                )
                P.op(
                    "DVE",
                    lambda e, kc=kc, oh=oh, ws=ws, st=st: e.tensor_mul(
                        W[ws][:, kc * D + oh * 512 : kc * D + oh * 512 + 512], scrq, st[:]
                    ),
                    reads=("scr0", f"SCT{sl}"), writes=(wres,), tag=f"dqm{tag}_{kc}_{oh}",
                )
        P.dma(
            "DMA_G",
            LA[ws][:].rearrange("p (c r) -> p c r", r=32),
            dram["l0f_laT"].rearrange("l (c p) r -> l p c r", p=128)[lin],
            writes=(f"LAF{ws}",), tag=f"la{tag}",
        )
        P.dma(
            "DMA_G", LB[ws][:], dram["l0f_lbT_pad"][lin], writes=(f"LBF{ws}",), tag=f"lb{tag}"
        )
        P.dma(
            "DMA_G", BIASF[ws][:], dram["l0f_b_pad"][lin],
            writes=(f"BIASF{ws}",), tag=f"bf{tag}",
        )

    def lora_u1(ws, src_ap, src_res, tc, tag):
        for kc in range(NC):
            lhsT = LA[ws][:, kc * 32 : kc * 32 + 32]
            sap = src_ap(kc)
            P.op("PE", lambda e, l=lhsT: e.nop() if True else e.ldweights(l), reads=(f"LAF{ws}",), tag=f"ldla{tag}_{kc}")
            P.op(
                "PE",
                lambda e, l=lhsT, s=sap, kc=kc: e.matmul(
                    B.pb[3][0:32, :], l, s, start=(kc == 0), stop=(kc == NC - 1)
                ),
                reads=(f"LAF{ws}", src_res(kc), "pb3"), writes=("pb3",), tag=f"mmla{tag}_{kc}",
            )
        P.op(
            "ACT",
            lambda e: e.activation(U1[:], B.pb[3][0:32, :], AF.Copy),
            reads=("pb3",), writes=("U1",), tag=f"evu1{tag}",
        )

    def linear_group(ws, oc, src_ap, src_res, bank, tag):
        pres = f"pb{bank}"
        pp = B.pb[bank]
        for kc in range(NC):
            lhsT = W[ws][:, kc * D + oc * 128 : kc * D + oc * 128 + 128]
            sap = src_ap(kc)
            P.op("PE", lambda e, l=lhsT: e.nop() if True else e.ldweights(l), reads=(f"WF{ws}",), tag=f"ldw{tag}_{kc}")
            P.op(
                "PE",
                lambda e, l=lhsT, s=sap, kc=kc, p=pp: e.matmul(
                    p[:, :], l, s, start=(kc == 0), stop=False
                ),
                reads=(f"WF{ws}", src_res(kc), pres), writes=(pres,), tag=f"mm{tag}_{kc}",
            )
        lbt = LB[ws][0:32, oc * 128 : oc * 128 + 128]
        P.op("PE", lambda e, l=lbt: e.nop() if True else e.ldweights(l), reads=(f"LBF{ws}",), tag=f"ldlb{tag}")
        P.op(
            "PE",
            lambda e, l=lbt, p=pp: e.matmul(p[:, :], l, U1[:], start=False, stop=False),
            reads=(f"LBF{ws}", "U1", pres), writes=(pres,), tag=f"mmlb{tag}",
        )
        brow = BIASF[ws][0:1, oc * 128 : oc * 128 + 128]
        P.op("PE", lambda e, b=brow: e.nop() if True else e.ldweights(b), reads=(f"BIASF{ws}",), tag=f"ldbb{tag}")
        P.op(
            "PE",
            lambda e, b=brow, p=pp: e.matmul(p[:, :], b, B.ones_b[:], start=False, stop=True),
            reads=(f"BIASF{ws}", "ones_b", pres), writes=(pres,), tag=f"mmbb{tag}",
        )
        return pp, pres

    dequant(0, "d0")
    for sb in range(3):
        linA, linB = 2 * sb, 2 * sb + 1
        wsA, wsB = linA % 2, linB % 2
        if sb == 0:
            hs_ap = lambda kc, tc: xs_chunk(B, kc, tc)
            hs_res = lambda kc, tc: f"XS{kc}_{tc}"
        else:
            hs_ap = lambda kc, tc: HB[:, kc * T + tc * 512 : kc * T + tc * 512 + 512]
            hs_res = lambda kc, tc: f"HB{kc}_{tc}"
        dequant(linB, f"d{linB}")
        for tc in range(NTC):
            lora_u1(wsA, lambda kc, tc=tc: hs_ap(kc, tc), lambda kc, tc=tc: hs_res(kc, tc), tc, f"A{sb}_{tc}")
            for oc in range(NC):
                bank = 5 if oc % 2 == 0 else 2
                pp, pres = linear_group(
                    wsA, oc,
                    lambda kc, tc=tc: hs_ap(kc, tc),
                    lambda kc, tc=tc: hs_res(kc, tc),
                    bank, f"y1_{sb}_{tc}_{oc}",
                )
                ydst = Y1[:, oc * 512 : oc * 512 + 512]
                P.op(
                    "ACT",
                    lambda e, d=ydst, p=pp: e.activation(d, p[:, :], AF.Copy),
                    reads=(pres,), writes=(f"Y1_{oc}",), tag=f"evy1_{sb}_{tc}_{oc}",
                )
                P.op(
                    "ACT",
                    lambda e, p=pp: e.activation(B.sq2[:], p[:, :], AF.Square),
                    reads=(pres,), writes=("sq2",), tag=f"y1sq_{sb}_{tc}_{oc}",
                )
                emit_stats_mms(P, B, f"Y1_{oc}", ydst, "sq2", 0, 1, oc, f"f{sb}_{tc}_{oc}")
            mu = rowslot(B, 0)
            var = rowslot(B, 1)
            arow = rowslot(B, 2)
            brow2 = rowslot(B, 3)
            P.op("DVE", lambda e: e.tensor_scalar(mu, B.pb[0][0:1, :], 1.0 / D, None, ALU.mult),
                 reads=("pb0",), writes=("row0",), tag=f"fmu{sb}_{tc}")
            P.op("DVE", lambda e: e.tensor_scalar(var, B.pb[1][0:1, :], 1.0 / D, None, ALU.mult),
                 reads=("pb1",), writes=("row1",), tag=f"fmsq{sb}_{tc}")
            P.op("ACT", lambda e: e.activation(arow, mu, AF.Square), reads=("row0",), writes=("row2",), tag=f"fmusq{sb}_{tc}")
            P.op("DVE", lambda e: e.tensor_sub(var, var, arow), reads=("row1", "row2"), writes=("row1",), tag=f"fvar{sb}_{tc}")
            P.op("ACT", lambda e: e.activation(var, var, AF.Ln, bias=B.cpool[0:1, 0:1]),
                 reads=("row1", "cpool"), writes=("row1",), tag=f"fln{sb}_{tc}")
            P.op("ACT", lambda e: e.activation(arow, var, AF.Exp, scale=-0.5),
                 reads=("row1",), writes=("row2",), tag=f"frstd{sb}_{tc}")
            P.op("DVE", lambda e: e.tensor_mul(brow2, mu, arow), reads=("row0", "row2"), writes=("row3",), tag=f"fbrow{sb}_{tc}")
            P.op("PE", lambda e: e.matmul(B.pb[0][:, :], B.ones_f[0:1, 0:128], arow, start=True, stop=True),
                 reads=("row2", "ones_f", "pb0"), writes=("pb0",), tag=f"fbcA{sb}_{tc}")
            P.op("PE", lambda e: e.matmul(B.pb[1][:, :], B.ones_f[0:1, 0:128], brow2, start=True, stop=True),
                 reads=("row3", "ones_f", "pb1"), writes=("pb1",), tag=f"fbcB{sb}_{tc}")
            for oc in range(NC):
                ysrc = Y1[:, oc * 512 : oc * 512 + 512]
                scr0 = scrslot(B, 0)
                scr1 = scrslot(B, 1)
                P.op(
                    "DVE",
                    lambda e, y=ysrc: e.tensor_mul(scr0, y, B.pb[0][:, :]),
                    reads=(f"Y1_{oc}", "pb0"), writes=("scr0",), tag=f"fap1_{sb}_{tc}_{oc}",
                )
                P.op(
                    "DVE",
                    lambda e: e.tensor_sub(scr1, scr0, B.pb[1][:, :]),
                    reads=("scr0", "pb1"), writes=("scr0",), tag=f"fap2_{sb}_{tc}_{oc}",
                )
                gcol = GB[:, sb * 16 + oc : sb * 16 + oc + 1]
                bcol = GB[:, sb * 16 + 8 + oc : sb * 16 + 8 + oc + 1]
                zdst = Z[:, oc * 512 : oc * 512 + 512]
                P.op(
                    "ACT",
                    lambda e, z=zdst, g=gcol, bb=bcol: e.activation(
                        z, scr1, AF.Gelu, scale=g, bias=bb
                    ),
                    reads=("scr0", "GB"), writes=(f"Z_{oc}",), tag=f"fgelu{sb}_{tc}_{oc}",
                )
            lora_u1(
                wsB,
                lambda kc: Z[:, kc * 512 : kc * 512 + 512],
                lambda kc: f"Z_{kc}",
                tc, f"B{sb}_{tc}",
            )
            for oc in range(NC):
                bank = 5 if oc % 2 == 0 else 2
                pp, pres = linear_group(
                    wsB, oc,
                    lambda kc: Z[:, kc * 512 : kc * 512 + 512],
                    lambda kc: f"Z_{kc}",
                    bank, f"y2_{sb}_{tc}_{oc}",
                )
                hdst = HB[:, oc * T + tc * 512 : oc * T + tc * 512 + 512]
                if sb == 0:
                    P.op(
                        "DVE",
                        lambda e, h=hdst, p=pp, oc=oc, tc=tc: e.tensor_add(
                            h, p[:, :], xs_chunk(B, oc, tc)
                        ),
                        reads=(pres, f"XS{oc}_{tc}"), writes=(f"HB{oc}_{tc}",),
                        tag=f"hres{sb}_{tc}_{oc}",
                    )
                else:
                    P.op(
                        "DVE",
                        lambda e, h=hdst, p=pp: e.tensor_add(h, p[:, :], h),
                        reads=(pres, f"HB{oc}_{tc}"), writes=(f"HB{oc}_{tc}",),
                        tag=f"hres{sb}_{tc}_{oc}",
                    )
        if sb < 2:
            dequant(linA + 2, f"d{linA + 2}")

    for tc in range(NTC):
        for fc in range(NC):
            xc = xs_chunk(B, fc, tc)
            hc = HB[:, fc * T + tc * 512 : fc * T + tc * 512 + 512]
            P.op(
                "DVE",
                lambda e, x=xc, h=hc: e.tensor_add(x, h, x),
                reads=(f"HB{fc}_{tc}", f"XS{fc}_{tc}"), writes=(f"XS{fc}_{tc}",),
                tag=f"fres{tc}_{fc}",
            )
            P.op(
                "ACT",
                lambda e, x=xc: e.activation(B.sq2[:], x, AF.Square),
                reads=(f"XS{fc}_{tc}",), writes=("sq2",), tag=f"frsq{tc}_{fc}",
            )
            emit_stats_mms(P, B, f"XS{fc}_{tc}", xc, "sq2", 0, 1, fc, f"fo{tc}_{fc}")
        ln_apply(
            P, B, tc, 0, 1,
            lambda c, tc=tc: f"XS{c}_{tc}",
            lambda c, tc=tc: xs_chunk(B, c, tc),
            lambda c, tc=tc: f"XS{c}_{tc}",
            lambda c, tc=tc: xs_chunk(B, c, tc),
            1, f"fo{tc}",
        )




def build_l1_ffn(P, B, es, dram, xout):
    nc = P.nc
    R1 = es.enter_context(nc.sbuf_tensor("R1", [128, 32 * 512], BF16))
    R2 = es.enter_context(nc.sbuf_tensor("R2", [128, 32 * 512], BF16))
    NX = es.enter_context(nc.sbuf_tensor("NX", [128, NC * 512], BF16))
    K0S = [es.enter_context(nc.sbuf_tensor(f"K0S{i}", [128, NC * 128], BF16)) for i in range(2)]
    K1S = [es.enter_context(nc.sbuf_tensor(f"K1S{i}", [128, 32 * 128], BF16)) for i in range(2)]
    K2S = [es.enter_context(nc.sbuf_tensor(f"K2S{i}", [128, 32 * 128], BF16)) for i in range(2)]
    K1U = [es.enter_context(nc.sbuf_tensor(f"K1U{i}", [128, 32 * 128], U8)) for i in range(2)]
    RS = es.enter_context(nc.sbuf_tensor("RS", [1, 1024], BF16))
    OS = es.enter_context(nc.sbuf_tensor("OS", [128, 512], F32))
    SCC = es.enter_context(nc.sbuf_tensor("SCC", [128, 3], F32))

    P.dma("DMA_SP", SCC[:], dram["l1f_scol"], writes=("SCCr",), tag="scol")
    P.op("DVE", lambda e: e.tensor_scalar(SCC[:, 0:1], SCC[:, 0:1], 2.0, None, ALU.mult),
         reads=("SCCr",), writes=("SCCr",), tag="s0x2")
    P.op("DVE", lambda e: e.tensor_scalar(SCC[:, 1:2], SCC[:, 1:2], 4.0, None, ALU.mult),
         reads=("SCCr",), writes=("SCCr",), tag="s1x4")
    P.op("DVE", lambda e: e.tensor_scalar(SCC[:, 2:3], SCC[:, 2:3], 4.0, None, ALU.mult),
         reads=("SCCr",), writes=("SCCr",), tag="s2x4")

    k0t_r = dram["l1f_k0t"]
    k0b_r = dram["l1f_k0b"]
    k1_r = dram["l1f_k1t"]
    k2_r = dram["l1f_k2t"]
    xo = xout.rearrange("(c p) t -> c p t", p=128)

    def rowsum_neghalf(src_ap, src_res, nchunks, tag):
        for c in range(nchunks):
            P.op("PE", lambda e: e.nop() if True else e.ldweights(B.onescol[:, 0:1]), reads=("onescol",), tag=f"ldrs{tag}_{c}")
            P.op(
                "PE",
                lambda e, c=c: e.matmul(
                    B.pb[3][0:1, :], B.onescol[:, 0:1], src_ap(c),
                    start=(c == 0), stop=(c == nchunks - 1),
                ),
                reads=(src_res(c), "onescol", "pb3"), writes=("pb3",), tag=f"mmrs{tag}_{c}",
            )
        rsf = rowslot(B, 3)
        P.op(
            "ACT",
            lambda e: e.activation(rsf, B.pb[3][0:1, :], AF.Copy, scale=-0.5),
            reads=("pb3",), writes=("row3",), tag=f"evrs{tag}",
        )
        P.op(
            "DVE",
            lambda e: e.tensor_copy(RS[0:1, 0:512], rsf),
            reads=("row3",), writes=("RS",), tag=f"rshi{tag}",
        )
        P.op(
            "DVE",
            lambda e: e.tensor_sub(RS[0:1, 512:1024], rsf, RS[0:1, 0:512]),
            reads=("row3", "RS"), writes=("RS",), tag=f"rslo{tag}",
        )

    for tc in range(NTC):
        for kc in range(NC):
            P.op(
                "DVE",
                lambda e, kc=kc, tc=tc: e.tensor_scalar(
                    NX[:, kc * 512 : kc * 512 + 512], xs_chunk(B, kc, tc), -1.0, None, ALU.mult
                ),
                reads=(f"XS{kc}_{tc}",), writes=(f"NX{kc}",), tag=f"nx{tc}_{kc}",
            )
        for ft in range(32):
            sl = ft % 2
            bank = 5 if ft % 2 == 0 else 2
            pp = B.pb[bank]
            pres = f"pb{bank}"
            usl = ft % 2
            P.dma("DMA_SP", K1U[usl][:, 0 : NC * 128].rearrange("p (c f) -> p c f", f=128), k0t_r[ft],
                  writes=(f"K1Ua{usl}",), tag=f"k0t{tc}_{ft}")
            P.dma("DMA_SP", K1U[usl][:, NC * 128 : 2 * NC * 128].rearrange("p (c f) -> p c f", f=128), k0b_r[ft],
                  writes=(f"K1Ub{usl}",), tag=f"k0b{tc}_{ft}")
            P.op(
                "DVE" if ft % 2 == 0 else "POOL",
                lambda e, usl=usl, sl=sl: e.tensor_copy(K0S[sl][:], K1U[usl][:, 0 : NC * 128]),
                reads=(f"K1Ua{usl}",), writes=(f"K0Sa{sl}",), tag=f"cva{tc}_{ft}",
            )
            P.op(
                "POOL" if ft % 2 == 0 else "DVE",
                lambda e, usl=usl, sl=sl: e.tensor_copy(
                    K1S[sl][:, 0 : NC * 128], K1U[usl][:, NC * 128 : 2 * NC * 128]
                ),
                reads=(f"K1Ub{usl}",), writes=(f"K0Sb{sl}",), tag=f"cvb{tc}_{ft}",
            )
            for kc in range(NC):
                lhsT = K0S[sl][:, kc * 128 : kc * 128 + 128]
                P.op("PE", lambda e, l=lhsT: e.nop() if True else e.ldweights(l), reads=(f"K0Sa{sl}",), tag=f"ld0t{tc}_{ft}_{kc}")
                P.op(
                    "PE",
                    lambda e, l=lhsT, kc=kc, tc=tc, p=pp: e.matmul(
                        p[:, :], l, xs_chunk(B, kc, tc), start=(kc == 0), stop=False
                    ),
                    reads=(f"K0Sa{sl}", f"XS{kc}_{tc}", pres), writes=(pres,), tag=f"mm0t{tc}_{ft}_{kc}",
                )
            for kc in range(NC):
                lhsT = K1S[sl][:, kc * 128 : kc * 128 + 128]
                P.op("PE", lambda e, l=lhsT: e.nop() if True else e.ldweights(l), reads=(f"K0Sb{sl}",), tag=f"ld0b{tc}_{ft}_{kc}")
                P.op(
                    "PE",
                    lambda e, l=lhsT, kc=kc, p=pp: e.matmul(
                        p[:, :], l, NX[:, kc * 512 : kc * 512 + 512],
                        start=False, stop=(kc == NC - 1),
                    ),
                    reads=(f"K0Sb{sl}", f"NX{kc}", pres), writes=(pres,), tag=f"mm0b{tc}_{ft}_{kc}",
                )
            P.op(
                "ACT",
                lambda e, ft=ft, p=pp: e.activation(
                    R1[:, ft * 512 : ft * 512 + 512], p[:, :], AF.Relu, scale=SCC[:, 0:1]
                ),
                reads=(pres, "SCCr"), writes=(f"R1_{ft}",), tag=f"ev1{tc}_{ft}",
            )
        rowsum_neghalf(
            lambda c: R1[:, c * 512 : c * 512 + 512], lambda c: f"R1_{c}", 32, f"rs1_{tc}"
        )
        for ft in range(32):
            sl = ft % 2
            bank = 5 if ft % 2 == 0 else 2
            pp = B.pb[bank]
            pres = f"pb{bank}"
            usl = ft % 2
            P.dma("DMA_SP", K1U[usl][:].rearrange("p (c f) -> p c f", f=128), k1_r[ft],
                  writes=(f"K1Ua{usl}", f"K1Ub{usl}"), tag=f"k1{tc}_{ft}")
            for qq in range(2):
                P.op(
                    ("DVE", "POOL")[(ft + qq) % 2],
                    lambda e, usl=usl, sl=sl, qq=qq: e.tensor_copy(
                        K1S[sl][:, qq * 2048 : qq * 2048 + 2048],
                        K1U[usl][:, qq * 2048 : qq * 2048 + 2048],
                    ),
                    reads=(f"K1Ua{usl}",), writes=(f"K1S{sl}", f"K0Sb{sl}"), tag=f"cv1{tc}_{ft}_{qq}",
                )
            for kc in range(32):
                lhsT = K1S[sl][:, kc * 128 : kc * 128 + 128]
                P.op("PE", lambda e, l=lhsT: e.nop() if True else e.ldweights(l), reads=(f"K1S{sl}",), tag=f"ld1{tc}_{ft}_{kc}")
                P.op(
                    "PE",
                    lambda e, l=lhsT, kc=kc, p=pp: e.matmul(
                        p[:, :], l, R1[:, kc * 512 : kc * 512 + 512],
                        start=(kc == 0), stop=False,
                    ),
                    reads=(f"K1S{sl}", f"R1_{kc}", pres), writes=(pres,), tag=f"mm1{tc}_{ft}_{kc}",
                )
            P.op("PE", lambda e: e.nop() if True else e.ldweights(B.ones_b[0:1, 0:128]), reads=("ones_b",), tag=f"ldr1{tc}_{ft}")
            P.op(
                "PE",
                lambda e, p=pp: e.matmul(p[:, :], B.ones_b[0:1, 0:128], RS[0:1, 0:512], start=False, stop=False),
                reads=("ones_b", "RS", pres), writes=(pres,), tag=f"mmr1{tc}_{ft}",
            )
            P.op(
                "PE",
                lambda e, p=pp: e.matmul(p[:, :], B.ones_b[0:1, 0:128], RS[0:1, 512:1024], start=False, stop=True),
                reads=("ones_b", "RS", pres), writes=(pres,), tag=f"mmr1b{tc}_{ft}",
            )
            P.op(
                "ACT",
                lambda e, ft=ft, p=pp: e.activation(
                    R2[:, ft * 512 : ft * 512 + 512], p[:, :], AF.Relu, scale=SCC[:, 1:2]
                ),
                reads=(pres, "SCCr"), writes=(f"R2_{ft}",), tag=f"ev2{tc}_{ft}",
            )
        rowsum_neghalf(
            lambda c: R2[:, c * 512 : c * 512 + 512], lambda c: f"R2_{c}", 32, f"rs2_{tc}"
        )
        for oc in range(NC):
            sl = oc % 2
            bank = 5 if oc % 2 == 0 else 2
            pp = B.pb[bank]
            pres = f"pb{bank}"
            usl = oc % 2
            P.dma("DMA_SP", K1U[usl][:].rearrange("p (c f) -> p c f", f=128), k2_r[oc],
                  writes=(f"K1Ua{usl}", f"K1Ub{usl}"), tag=f"k2{tc}_{oc}")
            for qq in range(2):
                P.op(
                    ("DVE", "POOL")[(oc + qq) % 2],
                    lambda e, usl=usl, sl=sl, qq=qq: e.tensor_copy(
                        K2S[sl][:, qq * 2048 : qq * 2048 + 2048],
                        K1U[usl][:, qq * 2048 : qq * 2048 + 2048],
                    ),
                    reads=(f"K1Ua{usl}",), writes=(f"K2S{sl}",), tag=f"cv2{tc}_{oc}_{qq}",
                )
            for kc in range(32):
                lhsT = K2S[sl][:, kc * 128 : kc * 128 + 128]
                P.op("PE", lambda e, l=lhsT: e.nop() if True else e.ldweights(l), reads=(f"K2S{sl}",), tag=f"ld2{tc}_{oc}_{kc}")
                P.op(
                    "PE",
                    lambda e, l=lhsT, kc=kc, p=pp: e.matmul(
                        p[:, :], l, R2[:, kc * 512 : kc * 512 + 512],
                        start=(kc == 0), stop=False,
                    ),
                    reads=(f"K2S{sl}", f"R2_{kc}", pres), writes=(pres,), tag=f"mm2{tc}_{oc}_{kc}",
                )
            P.op("PE", lambda e: e.nop() if True else e.ldweights(B.ones_b[0:1, 0:128]), reads=("ones_b",), tag=f"ldr2{tc}_{oc}")
            P.op(
                "PE",
                lambda e, p=pp: e.matmul(p[:, :], B.ones_b[0:1, 0:128], RS[0:1, 0:512], start=False, stop=False),
                reads=("ones_b", "RS", pres), writes=(pres,), tag=f"mmr2{tc}_{oc}",
            )
            P.op(
                "PE",
                lambda e, p=pp: e.matmul(p[:, :], B.ones_b[0:1, 0:128], RS[0:1, 512:1024], start=False, stop=True),
                reads=("ones_b", "RS", pres), writes=(pres,), tag=f"mmr2b{tc}_{oc}",
            )
            scr = scrslot(B, 0)
            P.op(
                "ACT",
                lambda e, p=pp: e.activation(scr, p[:, :], AF.Copy, scale=SCC[:, 2:3]),
                reads=(pres, "SCCr"), writes=("scr0",), tag=f"ev3{tc}_{oc}",
            )
            xc = xs_chunk(B, oc, tc)
            P.op(
                "DVE",
                lambda e, x=xc: e.tensor_add(x, scr, x),
                reads=("scr0", f"XS{oc}_{tc}"), writes=(f"XS{oc}_{tc}",), tag=f"res2{tc}_{oc}",
            )
            P.op(
                "ACT",
                lambda e, x=xc: e.activation(B.sq2[:], x, AF.Square),
                reads=(f"XS{oc}_{tc}",), writes=("sq2",), tag=f"rsq2{tc}_{oc}",
            )
            emit_stats_mms(P, B, f"XS{oc}_{tc}", xc, "sq2", 0, 1, oc, f"l1f{tc}_{oc}")
        mu = rowslot(B, 0)
        var = rowslot(B, 1)
        arow = rowslot(B, 2)
        brow2 = rowslot(B, 3)
        P.op("DVE", lambda e: e.tensor_scalar(mu, B.pb[0][0:1, :], 1.0 / D, None, ALU.mult),
             reads=("pb0",), writes=("row0",), tag=f"zmu{tc}")
        P.op("DVE", lambda e: e.tensor_scalar(var, B.pb[1][0:1, :], 1.0 / D, None, ALU.mult),
             reads=("pb1",), writes=("row1",), tag=f"zmsq{tc}")
        P.op("ACT", lambda e: e.activation(arow, mu, AF.Square), reads=("row0",), writes=("row2",), tag=f"zmusq{tc}")
        P.op("DVE", lambda e: e.tensor_sub(var, var, arow), reads=("row1", "row2"), writes=("row1",), tag=f"zvar{tc}")
        P.op("ACT", lambda e: e.activation(var, var, AF.Ln, bias=B.cpool[0:1, 0:1]),
             reads=("row1", "cpool"), writes=("row1",), tag=f"zln{tc}")
        P.op("ACT", lambda e: e.activation(arow, var, AF.Exp, scale=-0.5),
             reads=("row1",), writes=("row2",), tag=f"zrstd{tc}")
        P.op("DVE", lambda e: e.tensor_mul(brow2, mu, arow), reads=("row0", "row2"), writes=("row3",), tag=f"zbrow{tc}")
        P.op("PE", lambda e: e.matmul(B.pb[3][:, :], B.ones_f[0:1, 0:128], arow, start=True, stop=True),
             reads=("row2", "ones_f", "pb3"), writes=("pb3",), tag=f"zbcA{tc}")
        P.op("PE", lambda e: e.matmul(B.pb[4][:, :], B.ones_f[0:1, 0:128], brow2, start=True, stop=True),
             reads=("row3", "ones_f", "pb4"), writes=("pb4",), tag=f"zbcB{tc}")
        scr = scrslot(B, 0)
        for c in range(NC):
            xc = xs_chunk(B, c, tc)
            P.op(
                "DVE",
                lambda e, x=xc: e.tensor_mul(scr, x, B.pb[3][:, :]),
                reads=(f"XS{c}_{tc}", "pb3"), writes=("scr0",), tag=f"zap1{tc}_{c}",
            )
            P.op(
                "DVE",
                lambda e: e.tensor_sub(OS[:], scr, B.pb[4][:, :]),
                reads=("scr0", "pb4"), writes=("OS",), tag=f"zap2{tc}_{c}",
            )
            P.dma(
                "DMA_SP",
                xo[c, :, tc * 512 : tc * 512 + 512],
                OS[:],
                reads=("OS",), writes=(f"xout{c}_{tc}",), tag=f"zout{c}_{tc}",
            )




def dtype_of(v):
    return {
        np.dtype(np.float32): F32,
        np.dtype(np.int32): I32,
        np.dtype(np.uint8): U8,
        np.dtype(np.bool_): U8,
    }[v.dtype]


def build_program(dram_specs, phases, debug_out=None):
    nc = bass.Bass("TRN2", target_bir_lowering=False)
    dram = {}
    for name, (shape, dt) in dram_specs.items():
        dram[name] = nc.dram_tensor(name, list(shape), dt, kind="ExternalInput").ap()
    xout = nc.dram_tensor("xout", [D, T], F32, kind="ExternalOutput").ap()
    global DBG_R
    if DBG:
        DBG_R = [nc.dram_tensor("rdbg", [D, T], BF16, kind="ExternalOutput").ap()]
    else:
        DBG_R = None

    with ExitStack() as es:
        P = Prog(nc)
        B = build_common(nc, es)
        sems = {}
        for e in ENGINES + DMA_QUEUES:
            sems[e] = es.enter_context(nc.semaphore(f"sem_{e}"))
        block = es.enter_context(nc.Block())

        load_common(P, B, dram)
        if "l0a" in phases:
            with ExitStack() as esp:
                build_l0_attn(P, B, esp, dram)
                P.barrier("bar_l0a")
                P.finalize_phase(block, sems)
        if "l0f" in phases:
            with ExitStack() as esp:
                build_l0_ffn(P, B, esp, dram)
                P.barrier("bar_l0f")
                P.finalize_phase(block, sems)
        if "l1a" in phases:
            with ExitStack() as esp:
                build_l1_attn(P, B, esp, dram)
                P.barrier("bar_l1a")
                P.finalize_phase(block, sems)
        if "l1f" in phases:
            with ExitStack() as esp:
                build_l1_ffn(P, B, esp, dram, xout)
                P.barrier("bar_l1f")
                P.finalize_phase(block, sems)
        xo = xout.rearrange("(c p) t -> c p t", p=128)
        if "l1f" in phases:
            P.barrier("bar_out")
            P.finalize_phase(block, sems)
            print("PROG:", P.stats())
            return nc, P
        for c in range(NC):
            for tc in range(NTC):
                slot = tc % 2
                scr = scrslot(B, slot)
                P.op(
                    "DVE",
                    lambda e, c=c, tc=tc, scr=scr: e.tensor_copy(scr, xs_chunk(B, c, tc)),
                    reads=(f"XS{c}_{tc}",), writes=("scr0",), tag=f"osc{c}_{tc}",
                )
                P.dma(
                    "DMA_SP",
                    xo[c, :, tc * 512 : tc * 512 + 512],
                    scr,
                    reads=("scr0",), writes=(f"xout{c}_{tc}",), tag=f"out{c}_{tc}",
                )
        P.barrier("bar_out")
        P.finalize_phase(block, sems)
    return nc, P


def kernel(**inputs):
    cores = [host_layout(core, inputs) for core in range(8)]
    specs = {k: (v.shape, dtype_of(v)) for k, v in cores[0].items()}
    nc, P = build_program(specs, phases=("l0a", "l0f", "l1a", "l1f"))
    res = run_bass_kernel_spmd(nc, cores, core_ids=list(range(8)))
    out = np.stack([np.asarray(r["xout"]).T for r in res.results])
    return out.astype(np.float32)


# version 06
# speedup vs baseline: 1.1870x; 1.1870x over previous
import numpy as np
from contextlib import ExitStack

import concourse.bass as bass
import concourse.mybir as mybir
from concourse.bass_utils import run_bass_kernel_spmd

DBG_R = None
DBG = False
F32 = mybir.dt.float32
BF16 = mybir.dt.bfloat16
I32 = mybir.dt.int32
U8 = mybir.dt.uint8
AF = mybir.ActivationFunctionType
ALU = mybir.AluOpType
AX = mybir.AxisListType

D = 1024
T = 4096
NH = 16
DH = 64
NC = D // 128
NTC = T // 512
EPS = 1e-5
NSLOT = 4


ENGINES = ("PE", "ACT", "DVE", "POOL", "SP")
DMA_QUEUES = ("DMA_SP", "DMA_ACT", "DMA_G")
ISSUER = {"DMA_SP": "SP", "DMA_ACT": "ACT", "DMA_G": "POOL"}


class Op:
    __slots__ = ("engine", "emit", "reads", "writes", "idx", "needs_inc", "deps", "tag")

    def __init__(self, engine, emit, reads, writes, idx, tag):
        self.engine = engine
        self.emit = emit
        self.reads = reads
        self.writes = writes
        self.idx = idx
        self.needs_inc = engine in DMA_QUEUES
        self.deps = []
        self.tag = tag


class Prog:
    def __init__(self, nc):
        self.nc = nc
        self.ops = []
        self.last_write = {}
        self.readers = {}
        self.cursor = 0
        self.last_barrier = None
        self.inc_counts = {e: 0 for e in ENGINES + DMA_QUEUES}
        self.inc_at = {}
        self.watermark = {}

    def op(self, engine, emit, reads=(), writes=(), tag=""):
        o = Op(engine, emit, tuple(reads), tuple(writes), len(self.ops), tag)
        deps = {}
        for r in o.reads:
            w = self.last_write.get(r)
            if w is not None:
                deps[w.idx] = w
        for r in o.writes:
            w = self.last_write.get(r)
            if w is not None:
                deps[w.idx] = w
            for rd in self.readers.get(r, ()):
                deps[rd.idx] = rd
        if self.last_barrier is not None:
            deps.setdefault(self.last_barrier.idx, self.last_barrier)
        for d in deps.values():
            if (
                d.engine == o.engine
                and d.engine not in DMA_QUEUES
                and d.engine != "POOL"
            ):
                continue
            if d.idx < self.cursor and not d.needs_inc:
                raise RuntimeError(
                    f"op {o.tag} depends on already-emitted op {d.tag} without inc; "
                    "phases must end with a barrier"
                )
            o.deps.append(d)
        for r in o.writes:
            self.last_write[r] = o
            self.readers[r] = []
        for r in o.reads:
            self.readers.setdefault(r, []).append(o)
        self.ops.append(o)
        return o

    DOUBLE_DMA = True

    def dma(self, queue, out, in_, reads=(), writes=(), tag=""):
        assert out.shape[0] == 128, (tag, out.shape)

        def emit(eng, out=out, in_=in_):
            return eng.dma_start(out, in_)

        if self.DOUBLE_DMA:
            self.op(queue, emit, reads, writes, tag + "_pre")
        return self.op(queue, emit, reads, writes, tag)

    def barrier(self, tag="barrier"):
        lasts = {}
        for o in self.ops:
            lasts[o.engine] = o
        b = self.op("SP", lambda e: e.nop(), tag=tag)
        b.needs_inc = True
        for eng, o in lasts.items():
            if o is not b and not (o.engine == "SP"):
                b.deps.append(o)
        for r in list(self.last_write.keys()):
            self.last_write[r] = b
            self.readers[r] = []
        self.last_barrier = b
        return b

    def finalize_phase(self, block, sems):
        new_ops = self.ops[self.cursor :]
        for o in new_ops:
            best = {}
            for d in o.deps:
                if d.engine not in best or best[d.engine].idx < d.idx:
                    best[d.engine] = d
            o.deps = list(best.values())
            for d in o.deps:
                d.needs_inc = True
        for o in new_ops:
            if o.needs_inc:
                self.inc_counts[o.engine] += 16 if o.engine in DMA_QUEUES else 1
            self.inc_at[o.idx] = self.inc_counts[o.engine]
        by_stream = {}
        for o in new_ops:
            stream = ISSUER.get(o.engine, o.engine)
            by_stream.setdefault(stream, []).append(o)

        def make_body(stream, ops):
            def body(eng):
                for o in ops:
                    for d in sorted(o.deps, key=lambda d: d.idx):
                        val = self.inc_at[d.idx]
                        key = (stream, d.engine)
                        if val <= self.watermark.get(key, 0):
                            continue
                        eng.wait_ge(sems[d.engine], val)
                        self.watermark[key] = val
                    inst = o.emit(eng)
                    if o.needs_inc:
                        if inst is None:
                            raise RuntimeError(f"op {o.tag} needs inc, emitted None")
                        inst.then_inc(
                            sems[o.engine], 16 if o.engine in DMA_QUEUES else 1
                        )

            return body

        attr = {
            "SP": "sync",
            "POOL": "gpsimd",
            "PE": "tensor",
            "ACT": "scalar",
            "DVE": "vector",
        }
        for stream in ("SP", "POOL", "PE", "ACT", "DVE"):
            if stream in by_stream:
                getattr(block, attr[stream])(make_body(stream, by_stream[stream]))
        self.cursor = len(self.ops)

    def stats(self):
        from collections import Counter

        c = Counter(o.engine for o in self.ops)
        ninc = sum(1 for o in self.ops if o.needs_inc)
        ndep = sum(len(o.deps) for o in self.ops)
        return f"ops={len(self.ops)} incs={ninc} deps={ndep} {dict(c)}"




def host_layout(core, I):
    d = {}
    x = np.ascontiguousarray(np.asarray(I["L0_attn_x"][core]))
    d["xT"] = np.ascontiguousarray(x.T)

    wq = np.asarray(I["L0_attn_w_qkv"])
    d["l0_wqkv_h"] = np.ascontiguousarray(
        np.stack(
            [
                np.concatenate(
                    [wq[:, h * 64 : h * 64 + 64],
                     wq[:, D + h * 64 : D + h * 64 + 64],
                     wq[:, 2 * D + h * 64 : 2 * D + h * 64 + 64]],
                    axis=1,
                )
                for h in range(NH)
            ]
        )
    )
    bq = np.asarray(I["L0_attn_b_qkv"])
    bqh = np.stack(
        [
            np.concatenate(
                [bq[h * 64 : h * 64 + 64],
                 bq[D + h * 64 : D + h * 64 + 64],
                 bq[2 * D + h * 64 : 2 * D + h * 64 + 64]]
            )
            for h in range(NH)
        ]
    )
    biasrow = np.concatenate([bqh.reshape(-1), np.asarray(I["L0_attn_b_proj"])])
    d["l0_bias_pad"] = np.ascontiguousarray(
        np.broadcast_to(biasrow[None, :], (128, NH * 192 + D))
    )
    d["l0_wproj"] = np.ascontiguousarray(np.asarray(I["L0_attn_w_proj"]))

    kafull = np.zeros((128, T), np.float32)
    kafull[0] = 1.0
    d["l0_ka_full"] = kafull

    qw = np.asarray(I["L0_ffn_qweights"])
    d["l0f_qT"] = np.ascontiguousarray(np.transpose(qw, (0, 2, 1)))
    sc = np.asarray(I["L0_ffn_scales"])
    d["l0f_sT"] = np.ascontiguousarray(
        np.transpose(np.repeat(sc, 16, axis=2), (0, 2, 1))
    )
    d["l0f_b_pad"] = np.ascontiguousarray(
        np.broadcast_to(np.asarray(I["L0_ffn_biases"])[:, None, :], (6, 128, D))
    )
    la = np.asarray(I["L0_ffn_lora_a"])
    d["l0f_laT"] = np.ascontiguousarray(np.transpose(la, (0, 2, 1)))
    lb = np.asarray(I["L0_ffn_lora_b"])
    lbT = np.transpose(lb, (0, 2, 1))
    d["l0f_lbT_pad"] = np.ascontiguousarray(
        np.concatenate([lbT, np.zeros((6, 96, D), lbT.dtype)], axis=1)
    )
    lg = np.asarray(I["L0_ffn_ln_gamma"])
    lbta = np.asarray(I["L0_ffn_ln_beta"])
    cols = []
    for i in range(3):
        cols.append(lg[i].reshape(NC, 128).T)
        cols.append(lbta[i].reshape(NC, 128).T)
    d["l0f_gb"] = np.ascontiguousarray(np.concatenate(cols, axis=1))

    wv = np.asarray(I["L1_attn_w_v"])
    wwqk = np.asarray(I["L1_attn_w_wqk"])
    wqk = np.asarray(I["L1_attn_w_qk"])
    d["l1_w5_h"] = np.ascontiguousarray(
        np.stack(
            [
                np.concatenate(
                    [
                        wv[:, h * 64 : h * 64 + 64],
                        wwqk[:, h * 64 : h * 64 + 64],
                        wwqk[:, D + h * 64 : D + h * 64 + 64],
                        wqk[:, h * 64 : h * 64 + 64],
                        wqk[:, D + h * 64 : D + h * 64 + 64],
                    ],
                    axis=1,
                )
                for h in range(NH)
            ]
        )
    )
    bv = np.asarray(I["L1_attn_b_v"])
    bwqk = np.asarray(I["L1_attn_b_wqk"])
    bqk = np.asarray(I["L1_attn_b_qk"])
    b5h = np.stack(
        [
            np.concatenate(
                [
                    bv[h * 64 : h * 64 + 64],
                    bwqk[h * 64 : h * 64 + 64],
                    bwqk[D + h * 64 : D + h * 64 + 64],
                    bqk[h * 64 : h * 64 + 64],
                    bqk[D + h * 64 : D + h * 64 + 64],
                ]
            )
            for h in range(NH)
        ]
    )
    d["l1_b5_pad"] = np.ascontiguousarray(
        np.broadcast_to(b5h[:, None, :], (NH, 128, 320))
    )
    d["l1_wout"] = np.ascontiguousarray(np.asarray(I["L1_attn_w_out"]))
    d["l1_bout_pad"] = np.ascontiguousarray(
        np.broadcast_to(np.asarray(I["L1_attn_b_out"])[None, :], (128, D))
    )
    wpe = np.asarray(I["L1_attn_w_pos_embedding"])
    pe = np.asarray(I["L1_attn_pos_embedding"])
    d["l1_wbias"] = np.ascontiguousarray(np.tile(wpe[63 - np.arange(64)], 64)).astype(
        np.float32
    )
    d["l1_gbias"] = np.ascontiguousarray(
        pe[np.clip(T - 1 - np.arange(T), 0, 4094)]
    ).astype(np.float32)
    half = np.arange(T) % 128 >= 64
    ind_a = (~half).astype(np.float32)
    ind_b = half.astype(np.float32)
    qm_a = (-100.0 * ~half).astype(np.float32)
    qm_b = (-100.0 * half).astype(np.float32)
    ones_row = np.ones(T, np.float32)
    zeros_row = np.zeros(T, np.float32)

    def padfull(rows):
        a = np.zeros((128, T), np.float32)
        for i, r in enumerate(rows):
            a[i] = r
        return np.ascontiguousarray(a)

    d["l1_kg_full"] = padfull([ones_row, d["l1_gbias"]])
    d["l1_kw_full"] = padfull([ones_row, d["l1_wbias"], ind_a, ind_b])
    d["l1_qg_full"] = padfull([zeros_row, ones_row])
    d["l1_qw_full"] = padfull([zeros_row, ones_row, qm_b, qm_a])
    d["l1_gb_pad"] = np.ascontiguousarray(
        np.broadcast_to(
            np.stack([d["l1_gbias"], d["l1_wbias"]])[:, None, :], (2, 128, T)
        )
    )
    del d["l1_gbias"], d["l1_wbias"]

    k0u = np.asarray(I["L1_ffn_k0"]).astype(np.uint8)
    k1u = np.asarray(I["L1_ffn_k1"]).astype(np.uint8)
    k2u = np.asarray(I["L1_ffn_k2"]).astype(np.uint8)
    d["l1f_k0t"] = np.ascontiguousarray(
        k0u[:D].reshape(8, 128, 32, 128).transpose(2, 1, 0, 3)
    )
    d["l1f_k0b"] = np.ascontiguousarray(
        k0u[D:].reshape(8, 128, 32, 128).transpose(2, 1, 0, 3)
    )
    d["l1f_k1t"] = np.ascontiguousarray(
        k1u.reshape(32, 128, 32, 128).transpose(2, 1, 0, 3)
    )
    d["l1f_k2t"] = np.ascontiguousarray(
        k2u.reshape(32, 128, 8, 128).transpose(2, 1, 0, 3)
    )
    d["l1f_scol"] = np.ascontiguousarray(
        np.broadcast_to(
            np.stack([
                np.asarray(I["L1_ffn_s0"], np.float32),
                np.asarray(I["L1_ffn_s1"], np.float32),
                np.asarray(I["L1_ffn_s2"], np.float32),
            ])[None, :],
            (128, 3),
        )
    )

    d["c_ident"] = np.eye(128, dtype=np.float32)
    d["c_pool"] = np.ascontiguousarray(
        np.broadcast_to(
            np.array([EPS, 30.0, -30.0, 1.0 / D, 0, 0, 0, 0], np.float32)[None, :],
            (128, 8),
        )
    )
    return d




class Bufs:
    pass


def build_common(nc, es):
    B = Bufs()
    B.XS = es.enter_context(nc.sbuf_tensor("XS", [128, NC * T], BF16))
    B.cpool = es.enter_context(nc.sbuf_tensor("cpool", [128, 8], F32))
    B.identb = es.enter_context(nc.sbuf_tensor("identb", [128, 128], BF16))
    B.ones_b = es.enter_context(nc.sbuf_tensor("ones_b", [1, 512], BF16))
    B.ones_f = es.enter_context(nc.sbuf_tensor("ones_f", [1, 128], F32))
    B.onescol = es.enter_context(nc.sbuf_tensor("onescol", [128, 1], BF16))
    B.rows_f = es.enter_context(nc.sbuf_tensor("rows_f", [1, 2080], F32))
    B.scr = es.enter_context(nc.sbuf_tensor("scr", [128, 512], F32))
    B.sq2 = es.enter_context(nc.sbuf_tensor("sq2", [128, 512], BF16))
    B.pb = [es.enter_context(nc.psum_tensor(f"pb{i}", [128, 512], F32)) for i in range(6)]
    B.pbT = [es.enter_context(nc.psum_tensor(f"pbT{i}", [128, 512], BF16)) for i in range(2)]
    return B


def rowslot(B, i):
    return B.rows_f[0:1, i * 512 : (i + 1) * 512]


def rowcell(B, i):
    return B.rows_f[0:1, 2048 + i : 2049 + i]


def scrslot(B, i, rows=128):
    return B.scr[0:rows, 0:512]


def xs_chunk(B, c, tc):
    return B.XS[:, c * T + tc * 512 : c * T + tc * 512 + 512]


def load_common(P, B, dram):
    P.dma("DMA_SP", B.cpool[:], dram["c_pool"], writes=("cpool",), tag="cpool")
    P.dma("DMA_G", B.identb[:], dram["c_ident"], writes=("identb",), tag="ident")
    P.op("POOL", lambda e: e.memset(B.ones_b[:], 1.0), writes=("ones_b",))
    P.op("POOL", lambda e: e.memset(B.ones_f[:], 1.0), writes=("ones_f",))
    P.op("POOL", lambda e: e.memset(B.onescol[:], 1.0), writes=("onescol",))
    xr = dram["xT"].rearrange("(c p) t -> c p t", p=128)
    for c in range(NC):
        P.dma(
            "DMA_G",
            B.XS[:, c * T : (c + 1) * T],
            xr[c],
            writes=tuple(f"XS{c}_{tc}" for tc in range(NTC)),
            tag=f"loadx{c}",
        )


def emit_stats_mms(P, B, xres, xc, sqres, bankA, bankB, fc, tag):
    P.op("PE", lambda e: e.nop() if True else e.ldweights(B.onescol[:, 0:1]), reads=("onescol",), tag=f"ldst{tag}")
    P.op(
        "PE",
        lambda e: e.matmul(
            B.pb[bankA][0:1, :], B.onescol[:, 0:1], xc, start=(fc == 0), stop=(fc == NC - 1)
        ),
        reads=(xres, "onescol", f"pb{bankA}"),
        writes=(f"pb{bankA}",),
        tag=f"mmsA{tag}",
    )
    P.op(
        "PE",
        lambda e: e.matmul(
            B.pb[bankB][0:1, :], B.onescol[:, 0:1], B.sq2[:], start=(fc == 0), stop=(fc == NC - 1)
        ),
        reads=(sqres, "onescol", f"pb{bankB}"),
        writes=(f"pb{bankB}",),
        tag=f"mmsB{tag}",
    )


def ln_apply(P, B, tc, bankA, bankB, src_res, src_ap, out_res, out_ap, out_scr_slot, tag, bcA=3, bcB=4):
    mu = rowslot(B, 0)
    var = rowslot(B, 1)
    arow = rowslot(B, 2)
    brow = rowslot(B, 3)
    rA, rB = f"pb{bankA}", f"pb{bankB}"
    P.op(
        "DVE",
        lambda e: e.tensor_scalar(mu, B.pb[bankA][0:1, :], 1.0 / D, None, ALU.mult),
        reads=(rA,), writes=("row0",), tag=f"mu{tag}",
    )
    P.op(
        "DVE",
        lambda e: e.tensor_scalar(var, B.pb[bankB][0:1, :], 1.0 / D, None, ALU.mult),
        reads=(rB,), writes=("row1",), tag=f"msq{tag}",
    )
    P.op("ACT", lambda e: e.activation(arow, mu, AF.Square), reads=("row0",), writes=("row2",), tag=f"musq{tag}")
    P.op("DVE", lambda e: e.tensor_sub(var, var, arow), reads=("row1", "row2"), writes=("row1",), tag=f"var{tag}")
    P.op(
        "ACT",
        lambda e: e.activation(var, var, AF.Ln, bias=B.cpool[0:1, 0:1]),
        reads=("row1", "cpool"), writes=("row1",), tag=f"lnv{tag}",
    )
    P.op(
        "ACT",
        lambda e: e.activation(arow, var, AF.Exp, scale=-0.5),
        reads=("row1",), writes=("row2",), tag=f"rstd{tag}",
    )
    P.op("DVE", lambda e: e.tensor_mul(brow, mu, arow), reads=("row0", "row2"), writes=("row3",), tag=f"brow{tag}")
    rbA, rbB = f"pb{bcA}", f"pb{bcB}"
    P.op(
        "PE",
        lambda e: e.matmul(B.pb[bcA][:, :], B.ones_f[0:1, 0:128], arow, start=True, stop=True),
        reads=("row2", "ones_f", rbA), writes=(rbA,), tag=f"bcA{tag}",
    )
    P.op(
        "PE",
        lambda e: e.matmul(B.pb[bcB][:, :], B.ones_f[0:1, 0:128], brow, start=True, stop=True),
        reads=("row3", "ones_f", rbB), writes=(rbB,), tag=f"bcB{tag}",
    )
    scr = scrslot(B, out_scr_slot)
    for c in range(NC):
        sap = src_ap(c)
        oap = out_ap(c)
        P.op(
            "DVE",
            lambda e, sap=sap: e.tensor_mul(scr, sap, B.pb[bcA][:, :]),
            reads=(src_res(c), rbA), writes=("scr0",), tag=f"ap1{tag}_{c}",
        )
        P.op(
            "DVE",
            lambda e, oap=oap: e.tensor_sub(oap, scr, B.pb[bcB][:, :]),
            reads=("scr0", rbB), writes=(out_res(c),), tag=f"ap2{tag}_{c}",
        )




def build_l0_attn(P, B, es, dram):
    nc = P.nc
    global DBG_R
    OT = es.enter_context(nc.sbuf_tensor("OT", [128, NC * T], BF16))
    Qa = es.enter_context(nc.sbuf_tensor("Qa", [128, T], BF16))
    Ka = es.enter_context(nc.sbuf_tensor("Ka", [128, T], BF16))
    Va = es.enter_context(nc.sbuf_tensor("Va", [128, 32 * 65], BF16))
    VF = es.enter_context(nc.sbuf_tensor("VF", [64, 512], BF16))
    WQKV = [es.enter_context(nc.sbuf_tensor(f"WQKV{i}", [128, NC * 192], BF16)) for i in range(NSLOT)]
    WP = es.enter_context(nc.sbuf_tensor("WPL0", [128, NC * 1024], BF16))
    PT = [es.enter_context(nc.sbuf_tensor(f"PT{i}", [128, 512], BF16)) for i in range(3)]
    SQ = es.enter_context(nc.sbuf_tensor("SQL0", [64, 512], BF16))
    BIAS = es.enter_context(nc.sbuf_tensor("BIASL0", [128, NH * 192 + D], BF16))

    P.op("POOL", lambda e: e.memset(OT[:], 0.0), writes=tuple(f"OT{c}" for c in range(NC)))
    P.op("POOL", lambda e: e.memset(Qa[:], 0.0), writes=("Qa_z",) + tuple(f"Qa_q{t}" for t in range(NTC)) + tuple(f"Qa_m{t}" for t in range(NTC)))
    P.dma(
        "DMA_G", Ka[:], dram["l0_ka_full"],
        writes=("Ka_z", "Ka_ones") + tuple(f"Ka_k{t}" for t in range(NTC)),
        tag="kafull",
    )
    P.op(
        "POOL",
        lambda e: e.memset(bass.AP(Va, 64, [[32 * 65, 128], [65, 32], [1, 1]]), 1.0),
        writes=("Va_ones",),
    )
    P.dma(
        "DMA_G", BIAS[:], dram["l0_bias_pad"],
        writes=("bias_qkv", "bias_proj"), tag="biaspad",
    )
    P.dma(
        "DMA_G", WP[:].rearrange("p (c f) -> p c f", f=1024),
        dram["l0_wproj"].rearrange("(c p) f -> p c f", p=128),
        writes=("WPl0",), tag="wproj",
    )

    maxkneg = rowcell(B, 0)
    mk_tmp = rowcell(B, 1)
    normrow = rowslot(B, 0)

    def proj_tensor(h, ws, which, tc, bank, tag):
        pp = B.pb[bank]
        pres = f"pb{bank}"
        off = which * 64
        for kc in range(NC):
            lhsT = WQKV[ws][:, kc * 192 + off : kc * 192 + off + 64]
            P.op("PE", lambda e, l=lhsT: e.nop() if True else e.ldweights(l), reads=(f"WQKVs{ws}",), tag=f"ldw{tag}_{kc}")
            P.op(
                "PE",
                lambda e, l=lhsT, kc=kc: e.matmul(
                    pp[0:64, :], l, xs_chunk(B, kc, tc), start=(kc == 0), stop=False
                ),
                reads=(f"WQKVs{ws}", f"XS{kc}_{tc}", pres),
                writes=(pres,),
                tag=f"mm{tag}_{kc}",
            )
        brow = BIAS[0:1, h * 192 + off : h * 192 + off + 64]
        P.op("PE", lambda e, b=brow: e.nop() if True else e.ldweights(b), reads=("bias_qkv",), tag=f"ldwb{tag}")
        P.op(
            "PE",
            lambda e, b=brow: e.matmul(pp[0:64, :], b, B.ones_b[:], start=False, stop=True),
            reads=("bias_qkv", "ones_b", pres),
            writes=(pres,),
            tag=f"mmb{tag}",
        )
        return pp, pres

    for h in range(NH):
        ws = h % NSLOT
        P.dma(
            "DMA_G",
            WQKV[ws][:].rearrange("p (c f) -> p c f", f=192),
            dram["l0_wqkv_h"].rearrange("h (c p) f -> h p c f", p=128)[h],
            writes=(f"WQKVs{ws}",),
            tag=f"wqkv{h}",
        )
        for tc in range(NTC):
            pp, pres = proj_tensor(h, ws, 1, tc, 5 if tc % 2 == 0 else 0, f"k{h}_{tc}")
            P.op(
                "ACT",
                lambda e, tc=tc, p=pp: e.activation(Ka[64:128, tc * 512 : tc * 512 + 512], p[0:64, :], AF.Copy),
                reads=(pres,), writes=(f"Ka_k{tc}",), tag=f"evK{h}_{tc}",
            )
            P.op(
                "ACT",
                lambda e, p=pp: e.activation(SQ[:], p[0:64, :], AF.Square),
                reads=(pres,), writes=("SQ",), tag=f"sqK{h}_{tc}",
            )
            P.op("PE", lambda e: e.nop() if True else e.ldweights(B.onescol[0:64, 0:1]), reads=("onescol",), tag=f"lnk{h}_{tc}")
            P.op(
                "PE",
                lambda e: e.matmul(B.pb[4][0:1, :], B.onescol[0:64, 0:1], SQ[:], start=True, stop=True),
                reads=("onescol", "SQ", "pb4"), writes=("pb4",), tag=f"mmnk{h}_{tc}",
            )
            P.op(
                "DVE",
                lambda e: e.reduce_max(mk_tmp, B.pb[4][0:1, :], axis=AX.X),
                reads=("pb4",), writes=("cell1",), tag=f"rmax{h}_{tc}",
            )
            if tc == 0:
                P.op("DVE", lambda e: e.tensor_copy(maxkneg, mk_tmp), reads=("cell1",), writes=("cell0",), tag=f"mk0_{h}")
            else:
                P.op(
                    "DVE",
                    lambda e: e.tensor_max(maxkneg, maxkneg, mk_tmp),
                    reads=("cell1", "cell0"), writes=("cell0",), tag=f"mka{h}_{tc}",
                )
        P.op("ACT", lambda e: e.activation(maxkneg, maxkneg, AF.Sqrt), reads=("cell0",), writes=("cell0",), tag=f"mks{h}")
        P.op(
            "DVE",
            lambda e: e.tensor_scalar(maxkneg, maxkneg, -1.0, None, ALU.mult),
            reads=("cell0",), writes=("cell0",), tag=f"mkn{h}",
        )
        for tc in range(NTC):
            pp, pres = proj_tensor(h, ws, 0, tc, 5 if tc % 2 == 0 else 0, f"q{h}_{tc}")
            P.op(
                "ACT",
                lambda e, tc=tc, p=pp: e.activation(
                    Qa[64:128, tc * 512 : tc * 512 + 512], p[0:64, :], AF.Copy, scale=0.125
                ),
                reads=(pres,), writes=(f"Qa_q{tc}",), tag=f"evQ{h}_{tc}",
            )
            P.op(
                "ACT",
                lambda e, p=pp: e.activation(SQ[:], p[0:64, :], AF.Square, scale=0.125),
                reads=(pres,), writes=("SQ",), tag=f"sqQ{h}_{tc}",
            )
            P.op("PE", lambda e: e.nop() if True else e.ldweights(B.onescol[0:64, 0:1]), reads=("onescol",), tag=f"lnq{h}_{tc}")
            P.op(
                "PE",
                lambda e: e.matmul(B.pb[4][0:1, :], B.onescol[0:64, 0:1], SQ[:], start=True, stop=True),
                reads=("onescol", "SQ", "pb4"), writes=("pb4",), tag=f"mmnq{h}_{tc}",
            )
            P.op(
                "ACT",
                lambda e: e.activation(normrow, B.pb[4][0:1, :], AF.Sqrt),
                reads=("pb4",), writes=("row0",), tag=f"nq{h}_{tc}",
            )
            P.op(
                "DVE",
                lambda e, tc=tc: e.tensor_scalar(
                    Qa[0:1, tc * 512 : tc * 512 + 512], normrow, maxkneg, 30.0, ALU.mult, ALU.add
                ),
                reads=("row0", "cell0"), writes=(f"Qa_m{tc}",), tag=f"mh{h}_{tc}",
            )
        for tc in range(NTC):
            pp, pres = proj_tensor(h, ws, 2, tc, 5 if tc % 2 == 0 else 0, f"v{h}_{tc}")
            P.op(
                "ACT",
                lambda e, p=pp: e.activation(VF[0:64, :], p[0:64, :], AF.Copy),
                reads=(pres,), writes=("VF",), tag=f"evV{h}_{tc}",
            )
            for j in range(4):
                jt = tc * 4 + j
                ps = jt % 2
                P.op(
                    "PE",
                    lambda e, j=j, ps=ps: e.transpose(
                        B.pbT[ps][:, 0:64],
                        VF[0:64, j * 128 : j * 128 + 128],
                        B.identb[0:64, 0:64],
                    ),
                    reads=("VF", "identb", f"pbT{ps}"), writes=(f"pbT{ps}",), tag=f"trV{h}_{jt}",
                )
                P.op(
                    "DVE",
                    lambda e, jt=jt, ps=ps: e.tensor_copy(
                        Va[:, jt * 65 : jt * 65 + 64], B.pbT[ps][:, 0:64]
                    ),
                    reads=(f"pbT{ps}",), writes=(f"Va{jt}",), tag=f"evT{h}_{jt}",
                )
        for si in range(NTC):
            icols = slice(si * 512, si * 512 + 512)
            for jt in range(32):
                bk = jt % 3
                bank = B.pb[bk]
                bres = f"pb{bk}"
                lhsT = Ka[:, jt * 128 : jt * 128 + 128]
                P.op(
                    "PE",
                    lambda e, l=lhsT: e.nop() if True else e.ldweights(l),
                    reads=("Ka_ones", f"Ka_k{jt // 4}"), tag=f"ldS{h}_{si}_{jt}",
                )
                P.op(
                    "PE",
                    lambda e, l=lhsT, b=bank, ic=icols: e.matmul(b[:, :], l, Qa[:, ic], start=True, stop=True),
                    reads=("Ka_ones", f"Ka_k{jt // 4}", f"Qa_m{si}", f"Qa_q{si}", "Qa_z", bres),
                    writes=(bres,), tag=f"mmS{h}_{si}_{jt}",
                )
                P.op(
                    "ACT",
                    lambda e, b=bank, pt=PT[bk]: e.activation(pt[:], b[:, :], AF.Exp),
                    reads=(bres, f"PT{bk}"), writes=(f"PT{bk}",), tag=f"exp{h}_{si}_{jt}",
                )
                P.op(
                    "PE",
                    lambda e, jt=jt: e.nop() if True else e.ldweights(Va[:, jt * 65 : jt * 65 + 65]),
                    reads=(f"Va{jt}", "Va_ones"), tag=f"ldV{h}_{si}_{jt}",
                )
                P.op(
                    "PE",
                    lambda e, jt=jt, pt=PT[bk]: e.matmul(
                        B.pb[3][0:65, :], Va[:, jt * 65 : jt * 65 + 65], pt[:],
                        start=(jt == 0), stop=(jt == 31),
                    ),
                    reads=(f"Va{jt}", "Va_ones", f"PT{bk}", "pb3"),
                    writes=("pb3",), tag=f"mmPV{h}_{si}_{jt}",
                )
            rl = rowslot(B, 1)
            P.op(
                "DVE",
                lambda e: e.reciprocal(rl, B.pb[3][64:65, :]),
                reads=("pb3",), writes=("row1",), tag=f"rec{h}_{si}",
            )
            P.op(
                "PE",
                lambda e: e.matmul(B.pb[4][0:64, :], B.ones_f[0:1, 0:64], rl, start=True, stop=True),
                reads=("row1", "ones_f", "pb4"), writes=("pb4",), tag=f"bcl{h}_{si}",
            )
            scr = scrslot(B, 0)
            P.op(
                "DVE",
                lambda e: e.tensor_copy(scr[0:64, :], B.pb[3][0:64, :]),
                reads=("pb3",), writes=("scr0",), tag=f"oev{h}_{si}",
            )
            osl = OT[(h % 2) * 64 : (h % 2) * 64 + 64,
                     (h // 2) * T + si * 512 : (h // 2) * T + si * 512 + 512]
            P.op(
                "DVE",
                lambda e, o=osl: e.tensor_mul(o, scr[0:64, :], B.pb[4][0:64, :]),
                reads=("scr0", "pb4"), writes=(f"OT{h // 2}",), tag=f"onr{h}_{si}",
            )

    for tc in range(NTC):
        for fc in range(NC):
            bank = 5 if fc % 2 == 0 else 2
            pp = B.pb[bank]
            pres = f"pb{bank}"
            for kc in range(NC):
                lhsT = WP[:, kc * 1024 + fc * 128 : kc * 1024 + fc * 128 + 128]
                P.op("PE", lambda e, l=lhsT: e.nop() if True else e.ldweights(l), reads=("WPl0",), tag=f"ldp{tc}_{fc}_{kc}")
                P.op(
                    "PE",
                    lambda e, l=lhsT, kc=kc, p=pp, tc=tc: e.matmul(
                        p[:, :], l, OT[:, kc * T + tc * 512 : kc * T + tc * 512 + 512],
                        start=(kc == 0), stop=False,
                    ),
                    reads=("WPl0", f"OT{kc}", pres), writes=(pres,), tag=f"mmp{tc}_{fc}_{kc}",
                )
            brow = BIAS[0:1, NH * 192 + fc * 128 : NH * 192 + fc * 128 + 128]
            P.op("PE", lambda e, b=brow: e.nop() if True else e.ldweights(b), reads=("bias_proj",), tag=f"ldbp{tc}_{fc}")
            P.op(
                "PE",
                lambda e, b=brow, p=pp: e.matmul(p[:, :], b, B.ones_b[:], start=False, stop=True),
                reads=("bias_proj", "ones_b", pres), writes=(pres,), tag=f"mmbp{tc}_{fc}",
            )
            xc = xs_chunk(B, fc, tc)
            P.op(
                "DVE",
                lambda e, x=xc, p=pp: e.tensor_add(x, p[:, :], x),
                reads=(pres, f"XS{fc}_{tc}"), writes=(f"XS{fc}_{tc}",), tag=f"res{tc}_{fc}",
            )
            P.op(
                "ACT",
                lambda e, x=xc: e.activation(B.sq2[:], x, AF.Square),
                reads=(f"XS{fc}_{tc}",), writes=("sq2",), tag=f"rsq{tc}_{fc}",
            )
            if DBG_R is not None:
                P.dma(
                    "DMA_SP",
                    DBG_R[0].rearrange("(c p) t -> c p t", p=128)[fc, :, tc * 512 : tc * 512 + 512],
                    xc,
                    reads=(f"XS{fc}_{tc}",), writes=(f"rdbg{fc}_{tc}",), tag=f"rdbg{tc}_{fc}",
                )
            emit_stats_mms(P, B, f"XS{fc}_{tc}", xc, "sq2", 0, 1, fc, f"l0a{tc}_{fc}")
        ln_apply(
            P, B, tc, 0, 1,
            lambda c, tc=tc: f"XS{c}_{tc}",
            lambda c, tc=tc: xs_chunk(B, c, tc),
            lambda c, tc=tc: f"XS{c}_{tc}",
            lambda c, tc=tc: xs_chunk(B, c, tc),
            1, f"l0a{tc}",
        )




def build_l1_attn(P, B, es, dram):
    nc = P.nc
    OT = es.enter_context(nc.sbuf_tensor("OT1", [128, NC * T], BF16))
    Qg = es.enter_context(nc.sbuf_tensor("Qg", [128, T], BF16))
    Kg = es.enter_context(nc.sbuf_tensor("Kg", [128, T], BF16))
    Qw = es.enter_context(nc.sbuf_tensor("Qw", [128, T], BF16))
    Kw = es.enter_context(nc.sbuf_tensor("Kw", [128, T], BF16))
    Va = es.enter_context(nc.sbuf_tensor("Va1", [128, 32 * 65], BF16))

    W5 = [es.enter_context(nc.sbuf_tensor(f"W5_{i}", [128, NC * 320], BF16)) for i in range(2)]
    WOS = [es.enter_context(nc.sbuf_tensor(f"WOS{i}", [128, NC * 128], BF16)) for i in range(2)]
    PT = [es.enter_context(nc.sbuf_tensor(f"PT1{i}", [128, 512], BF16)) for i in range(3)]
    PTW = es.enter_context(nc.sbuf_tensor("PTW", [128, 128], BF16))
    SQ = es.enter_context(nc.sbuf_tensor("SQL1", [64, 512], BF16))
    VF = SQ
    B5 = [es.enter_context(nc.sbuf_tensor(f"B5_{i}", [128, 320], BF16)) for i in range(2)]
    BOUT = es.enter_context(nc.sbuf_tensor("BOUT", [128, D], BF16))

    P.op("POOL", lambda e: e.memset(OT[:], 0.0), writes=tuple(f"OT{c}" for c in range(NC)))
    P.dma("DMA_G", Qg[:], dram["l1_qg_full"],
          writes=("Qg_z", "Qg_ones") + tuple(f"Qg_q{t}" for t in range(NTC)) + tuple(f"Qg_m{t}" for t in range(NTC)),
          tag="qgfull")
    P.dma("DMA_G", Kg[:], dram["l1_kg_full"],
          writes=("Kg_z", "Kg_ones", "Kg_g") + tuple(f"Kg_k{t}" for t in range(NTC)), tag="kgfull")
    P.dma("DMA_G", Qw[:], dram["l1_qw_full"],
          writes=("Qw_z", "Qw_ones", "Qw_mb", "Qw_ma") + tuple(f"Qw_q{t}" for t in range(NTC)) + tuple(f"Qw_m{t}" for t in range(NTC)),
          tag="qwfull")
    P.dma("DMA_G", Kw[:], dram["l1_kw_full"],
          writes=("Kw_z", "Kw_ones", "Kw_wb", "Kw_ia", "Kw_ib") + tuple(f"Kw_k{t}" for t in range(NTC)), tag="kwfull")

    P.op(
        "POOL",
        lambda e: e.memset(bass.AP(Va, 64, [[32 * 65, 128], [65, 32], [1, 1]]), 1.0),
        writes=("Va_ones",),
    )
    P.dma("DMA_G", BOUT[:], dram["l1_bout_pad"], writes=("bias_out",), tag="bout")

    c30mg = rowcell(B, 2)
    c30mw = rowcell(B, 3)
    gtmp = rowcell(B, 5)
    growslot = rowslot(B, 3)
    for name, cell, cres, dsrc in (
        ("g", c30mg, "cell2", "l1_gbias"),
        ("w", c30mw, "cell3", "l1_wbias"),
    ):
        gi = 0 if dsrc == "l1_gbias" else 1
        for ch in range(NTC):
            P.dma(
                "DMA_SP", scrslot(B, 0), dram["l1_gb_pad"][gi, :, ch * 512 : ch * 512 + 512],
                reads=("scr0",), writes=("scr0",), tag=f"gld{name}{ch}",
            )
            P.op("DVE", lambda e: e.reduce_max(gtmp, scrslot(B, 0)[0:1, :], axis=AX.X), reads=("scr0",), writes=("cell5",), tag=f"grm{name}{ch}")
            if ch == 0:
                P.op("DVE", lambda e, c=cell: e.tensor_copy(c, gtmp), reads=("cell5",), writes=(cres,), tag=f"gc{name}{ch}")
            else:
                P.op("DVE", lambda e, c=cell: e.tensor_max(c, c, gtmp), reads=("cell5", cres), writes=(cres,), tag=f"gm{name}{ch}")
        P.op("DVE", lambda e, c=cell: e.tensor_scalar(c, c, -1.0, 30.0, ALU.mult, ALU.add), reads=(cres,), writes=(cres,), tag=f"c30m{name}")

    mkg = rowcell(B, 0)
    mkw = rowcell(B, 1)
    mtmp = rowcell(B, 4)
    normrow = rowslot(B, 0)

    def proj5(h, ws, idx, tc, bank, tag):
        pp = B.pb[bank]
        pres = f"pb{bank}"
        off = idx * 64
        for kc in range(NC):
            lhsT = W5[ws][:, kc * 320 + off : kc * 320 + off + 64]
            P.op("PE", lambda e, l=lhsT: e.nop() if True else e.ldweights(l), reads=(f"W5s{ws}",), tag=f"ldw{tag}_{kc}")
            P.op(
                "PE",
                lambda e, l=lhsT, kc=kc, p=pp: e.matmul(
                    p[0:64, :], l, xs_chunk(B, kc, tc), start=(kc == 0), stop=False
                ),
                reads=(f"W5s{ws}", f"XS{kc}_{tc}", pres), writes=(pres,), tag=f"mm{tag}_{kc}",
            )
        brow = B5[ws][0:1, off : off + 64]
        P.op("PE", lambda e, b=brow: e.nop() if True else e.ldweights(b), reads=(f"B5s{ws}",), tag=f"ldb{tag}")
        P.op(
            "PE",
            lambda e, b=brow, p=pp: e.matmul(p[0:64, :], b, B.ones_b[:], start=False, stop=True),
            reads=(f"B5s{ws}", "ones_b", pres), writes=(pres,), tag=f"mmb{tag}",
        )
        return pp, pres

    def norm_path(pp, pres, scale, negcell, negres, c30, c30res, dstrow, dstres, tag):
        P.op(
            "ACT",
            lambda e: e.activation(SQ[:], pp[0:64, :], AF.Square, scale=scale),
            reads=(pres,), writes=("SQL1",), tag=f"sq{tag}",
        )
        P.op("PE", lambda e: e.nop() if True else e.ldweights(B.onescol[0:64, 0:1]), reads=("onescol",), tag=f"ldn{tag}")
        P.op(
            "PE",
            lambda e: e.matmul(B.pb[4][0:1, :], B.onescol[0:64, 0:1], SQ[:], start=True, stop=True),
            reads=("onescol", "SQL1", "pb4"), writes=("pb4",), tag=f"mmn{tag}",
        )
        P.op(
            "ACT",
            lambda e: e.activation(normrow, B.pb[4][0:1, :], AF.Sqrt),
            reads=("pb4",), writes=("row0",), tag=f"nr{tag}",
        )
        P.op(
            "DVE",
            lambda e, d=dstrow: e.tensor_scalar(d, normrow, negcell, c30, ALU.mult, ALU.add),
            reads=("row0", negres, c30res), writes=(dstres,), tag=f"mh{tag}",
        )

    def knorm_max(pp, pres, scale, negcell, negres, first, tag):
        P.op(
            "ACT",
            lambda e: e.activation(SQ[:], pp[0:64, :], AF.Square, scale=scale),
            reads=(pres,), writes=("SQL1",), tag=f"sq{tag}",
        )
        P.op("PE", lambda e: e.nop() if True else e.ldweights(B.onescol[0:64, 0:1]), reads=("onescol",), tag=f"ldn{tag}")
        P.op(
            "PE",
            lambda e: e.matmul(B.pb[4][0:1, :], B.onescol[0:64, 0:1], SQ[:], start=True, stop=True),
            reads=("onescol", "SQL1", "pb4"), writes=("pb4",), tag=f"mmn{tag}",
        )
        P.op("DVE", lambda e: e.reduce_max(mtmp, B.pb[4][0:1, :], axis=AX.X), reads=("pb4",), writes=("cell4",), tag=f"rm{tag}")
        if first:
            P.op("DVE", lambda e, n=negcell: e.tensor_copy(n, mtmp), reads=("cell4",), writes=(negres,), tag=f"mk0{tag}")
        else:
            P.op("DVE", lambda e, n=negcell: e.tensor_max(n, n, mtmp), reads=("cell4", negres), writes=(negres,), tag=f"mka{tag}")

    for h in range(NH):
        ws = h % 2
        P.dma(
            "DMA_G",
            W5[ws][:].rearrange("p (c f) -> p c f", f=320),
            dram["l1_w5_h"].rearrange("h (c p) f -> h p c f", p=128)[h],
            writes=(f"W5s{ws}",), tag=f"w5{h}",
        )
        P.dma("DMA_G", B5[ws][:], dram["l1_b5_pad"][h], writes=(f"B5s{ws}",), tag=f"b5{h}")
        for tc in range(NTC):
            pp, pres = proj5(h, ws, 4, tc, 5 if tc % 2 == 0 else 0, f"kg{h}_{tc}")
            P.op(
                "ACT",
                lambda e, tc=tc, p=pp: e.activation(Kg[64:128, tc * 512 : tc * 512 + 512], p[0:64, :], AF.Copy),
                reads=(pres,), writes=(f"Kg_k{tc}",), tag=f"evKg{h}_{tc}",
            )
            knorm_max(pp, pres, 1.0, mkg, "cell0", tc == 0, f"kg{h}_{tc}")
        for tc in range(NTC):
            pp, pres = proj5(h, ws, 2, tc, 5 if tc % 2 == 0 else 0, f"kw{h}_{tc}")
            P.op(
                "ACT",
                lambda e, tc=tc, p=pp: e.activation(Kw[64:128, tc * 512 : tc * 512 + 512], p[0:64, :], AF.Copy),
                reads=(pres,), writes=(f"Kw_k{tc}",), tag=f"evKw{h}_{tc}",
            )
            knorm_max(pp, pres, 1.0, mkw, "cell1", tc == 0, f"kw{h}_{tc}")
        for cell, cres, tg in ((mkg, "cell0", f"g{h}"), (mkw, "cell1", f"w{h}")):
            P.op("ACT", lambda e, c=cell: e.activation(c, c, AF.Sqrt), reads=(cres,), writes=(cres,), tag=f"mks{tg}")
            P.op(
                "DVE",
                lambda e, c=cell: e.tensor_scalar(c, c, -1.0, None, ALU.mult),
                reads=(cres,), writes=(cres,), tag=f"mkn{tg}",
            )
        for tc in range(NTC):
            pp, pres = proj5(h, ws, 3, tc, 5 if tc % 2 == 0 else 0, f"qg{h}_{tc}")
            P.op(
                "ACT",
                lambda e, tc=tc, p=pp: e.activation(Qg[64:128, tc * 512 : tc * 512 + 512], p[0:64, :], AF.Copy),
                reads=(pres,), writes=(f"Qg_q{tc}",), tag=f"evQg{h}_{tc}",
            )
            norm_path(
                pp, pres, 1.0, mkg, "cell0", c30mg, "cell2",
                Qg[0:1, tc * 512 : tc * 512 + 512], f"Qg_m{tc}", f"qg{h}_{tc}",
            )
        for tc in range(NTC):
            pp, pres = proj5(h, ws, 1, tc, 5 if tc % 2 == 0 else 0, f"qw{h}_{tc}")
            P.op(
                "ACT",
                lambda e, tc=tc, p=pp: e.activation(Qw[64:128, tc * 512 : tc * 512 + 512], p[0:64, :], AF.Copy, scale=0.125),
                reads=(pres,), writes=(f"Qw_q{tc}",), tag=f"evQw{h}_{tc}",
            )
            norm_path(
                pp, pres, 0.125, mkw, "cell1", c30mw, "cell3",
                Qw[0:1, tc * 512 : tc * 512 + 512], f"Qw_m{tc}", f"qw{h}_{tc}",
            )
        for tc in range(NTC):
            pp, pres = proj5(h, ws, 0, tc, 5 if tc % 2 == 0 else 0, f"v{h}_{tc}")
            P.op(
                "ACT",
                lambda e, p=pp: e.activation(VF[0:64, :], p[0:64, :], AF.Copy),
                reads=(pres,), writes=("SQL1",), tag=f"evV{h}_{tc}",
            )
            for j in range(4):
                jt = tc * 4 + j
                ps = jt % 2
                P.op(
                    "PE",
                    lambda e, j=j, ps=ps: e.transpose(
                        B.pbT[ps][:, 0:64], VF[0:64, j * 128 : j * 128 + 128],
                        B.identb[0:64, 0:64],
                    ),
                    reads=("SQL1", "identb", f"pbT{ps}"), writes=(f"pbT{ps}",), tag=f"trV1{h}_{jt}",
                )
                P.op(
                    "DVE",
                    lambda e, jt=jt, ps=ps: e.tensor_copy(
                        Va[:, jt * 65 : jt * 65 + 64], B.pbT[ps][:, 0:64]
                    ),
                    reads=(f"pbT{ps}",), writes=(f"Va{jt}",), tag=f"evT1{h}_{jt}",
                )
        for si in range(NTC):
            icols = slice(si * 512, si * 512 + 512)
            for jt in range(32):
                bk = jt % 3
                bank = B.pb[bk]
                bres = f"pb{bk}"
                lhsT = Kg[:, jt * 128 : jt * 128 + 128]
                P.op(
                    "PE",
                    lambda e, l=lhsT: e.nop() if True else e.ldweights(l),
                    reads=("Kg_ones", "Kg_g", f"Kg_k{jt // 4}"), tag=f"ldS1{h}_{si}_{jt}",
                )
                P.op(
                    "PE",
                    lambda e, l=lhsT, b=bank, ic=icols: e.matmul(b[:, :], l, Qg[:, ic], start=True, stop=True),
                    reads=("Kg_ones", "Kg_g", f"Kg_k{jt // 4}", f"Qg_m{si}", f"Qg_q{si}", "Qg_z", "Qg_ones", bres),
                    writes=(bres,), tag=f"mmS1{h}_{si}_{jt}",
                )
                P.op(
                    "ACT",
                    lambda e, b=bank, pt=PT[bk]: e.activation(pt[:], b[:, :], AF.Exp),
                    reads=(bres, f"PT1{bk}"), writes=(f"PT1{bk}",), tag=f"exp1{h}_{si}_{jt}",
                )
                P.op(
                    "PE",
                    lambda e, jt=jt: e.nop() if True else e.ldweights(Va[:, jt * 65 : jt * 65 + 65]),
                    reads=(f"Va{jt}", "Va_ones"), tag=f"ldV1{h}_{si}_{jt}",
                )
                P.op(
                    "PE",
                    lambda e, jt=jt, pt=PT[bk]: e.matmul(
                        B.pb[3][0:65, :], Va[:, jt * 65 : jt * 65 + 65], pt[:],
                        start=(jt == 0), stop=(jt == 31),
                    ),
                    reads=(f"Va{jt}", "Va_ones", f"PT1{bk}", "pb3"),
                    writes=("pb3",), tag=f"mmPV1{h}_{si}_{jt}",
                )
            for j in range(4):
                jt = si * 4 + j
                lhsT = Kw[:, jt * 128 : jt * 128 + 128]
                qcols = slice(jt * 128, jt * 128 + 128)
                P.op(
                    "PE",
                    lambda e, l=lhsT: e.nop() if True else e.ldweights(l),
                    reads=("Kw_ones", "Kw_wb", "Kw_ia", "Kw_ib", f"Kw_k{jt // 4}"),
                    tag=f"ldSw{h}_{si}_{j}",
                )
                P.op(
                    "PE",
                    lambda e, l=lhsT, qc=qcols: e.matmul(B.pb[2][:, 0:128], l, Qw[:, qc], start=True, stop=True),
                    reads=("Kw_ones", "Kw_wb", "Kw_ia", "Kw_ib", f"Kw_k{jt // 4}",
                           f"Qw_m{si}", f"Qw_q{si}", "Qw_z", "Qw_ones", "Qw_mb", "Qw_ma", "pb2"),
                    writes=("pb2",), tag=f"mmSw{h}_{si}_{j}",
                )
                P.op(
                    "ACT",
                    lambda e: e.activation(PTW[:], B.pb[2][:, 0:128], AF.Exp),
                    reads=("pb2", "PTW"), writes=("PTW",), tag=f"expw{h}_{si}_{j}",
                )
                P.op(
                    "PE",
                    lambda e, jt=jt: e.nop() if True else e.ldweights(Va[:, jt * 65 : jt * 65 + 65]),
                    reads=(f"Va{jt}", "Va_ones"), tag=f"ldVw{h}_{si}_{j}",
                )
                P.op(
                    "PE",
                    lambda e, jt=jt, j=j: e.matmul(
                        B.pb[5][0:65, j * 128 : j * 128 + 128],
                        Va[:, jt * 65 : jt * 65 + 65], PTW[:],
                        start=True, stop=True,
                    ),
                    reads=(f"Va{jt}", "Va_ones", "PTW", "pb5"),
                    writes=("pb5",), tag=f"mmPVw{h}_{si}_{j}",
                )
            rl = rowslot(B, 1)
            P.op("DVE", lambda e: e.reciprocal(rl, B.pb[3][64:65, :]), reads=("pb3",), writes=("row1",), tag=f"rec1{h}_{si}")
            P.op(
                "PE",
                lambda e: e.matmul(B.pb[4][0:64, :], B.ones_f[0:1, 0:64], rl, start=True, stop=True),
                reads=("row1", "ones_f", "pb4"), writes=("pb4",), tag=f"bcl1{h}_{si}",
            )
            scr = scrslot(B, 0)
            P.op("DVE", lambda e: e.tensor_copy(scr[0:64, :], B.pb[3][0:64, :]), reads=("pb3",), writes=("scr0",), tag=f"oev1{h}_{si}")
            osl = OT[(h % 2) * 64 : (h % 2) * 64 + 64,
                     (h // 2) * T + si * 512 : (h // 2) * T + si * 512 + 512]
            P.op(
                "DVE",
                lambda e, o=osl: e.tensor_mul(o, scr[0:64, :], B.pb[4][0:64, :]),
                reads=("scr0", "pb4"), writes=(f"OT{h // 2}",), tag=f"onr1{h}_{si}",
            )
            rlw = rowslot(B, 2)
            P.op("DVE", lambda e: e.reciprocal(rlw, B.pb[5][64:65, :]), reads=("pb5",), writes=("row2",), tag=f"recw{h}_{si}")
            P.op(
                "PE",
                lambda e: e.matmul(B.pb[4][0:64, :], B.ones_f[0:1, 0:64], rlw, start=True, stop=True),
                reads=("row2", "ones_f", "pb4"), writes=("pb4",), tag=f"bclw{h}_{si}",
            )
            b0 = (h % 2) * 64
            scr1 = scrslot(B, 1)
            P.op(
                "DVE",
                lambda e, b0=b0: e.tensor_copy(scr1[b0 : b0 + 64, :], B.pb[5][0:64, :]),
                reads=("pb5",), writes=("scr0",), tag=f"oevw{h}_{si}",
            )
            P.op(
                "DVE",
                lambda e, b0=b0: e.tensor_mul(
                    scr1[b0 : b0 + 64, :], scr1[b0 : b0 + 64, :], B.pb[4][0:64, :]
                ),
                reads=("scr0", "pb4"), writes=("scr0",), tag=f"onrw{h}_{si}",
            )
            P.op(
                "DVE",
                lambda e, o=osl, b0=b0: e.tensor_add(o, scr1[b0 : b0 + 64, :], o),
                reads=("scr0", f"OT{h // 2}"), writes=(f"OT{h // 2}",), tag=f"oadd{h}_{si}",
            )

    wo_r = dram["l1_wout"].rearrange("(c p) f -> p c f", p=128)
    for tc in range(NTC):
        for fc in range(NC):
            bank = 5 if fc % 2 == 0 else 2
            pp = B.pb[bank]
            pres = f"pb{bank}"
            wsl = fc % 2
            P.dma(
                "DMA_G",
                WOS[wsl][:].rearrange("p (c f) -> p c f", f=128),
                wo_r[:, :, fc * 128 : fc * 128 + 128],
                writes=(f"WOS{wsl}",), tag=f"wos{tc}_{fc}",
            )
            for kc in range(NC):
                lhsT = WOS[wsl][:, kc * 128 : kc * 128 + 128]
                P.op("PE", lambda e, l=lhsT: e.nop() if True else e.ldweights(l), reads=(f"WOS{wsl}",), tag=f"ldp1{tc}_{fc}_{kc}")
                P.op(
                    "PE",
                    lambda e, l=lhsT, kc=kc, p=pp, tc=tc: e.matmul(
                        p[:, :], l, OT[:, kc * T + tc * 512 : kc * T + tc * 512 + 512],
                        start=(kc == 0), stop=False,
                    ),
                    reads=(f"WOS{wsl}", f"OT{kc}", pres), writes=(pres,), tag=f"mmp1{tc}_{fc}_{kc}",
                )
            brow = BOUT[0:1, fc * 128 : fc * 128 + 128]
            P.op("PE", lambda e, b=brow: e.nop() if True else e.ldweights(b), reads=("bias_out",), tag=f"ldbp1{tc}_{fc}")
            P.op(
                "PE",
                lambda e, b=brow, p=pp: e.matmul(p[:, :], b, B.ones_b[:], start=False, stop=True),
                reads=("bias_out", "ones_b", pres), writes=(pres,), tag=f"mmbp1{tc}_{fc}",
            )
            xc = xs_chunk(B, fc, tc)
            P.op(
                "DVE",
                lambda e, x=xc, p=pp: e.tensor_add(x, p[:, :], x),
                reads=(pres, f"XS{fc}_{tc}"), writes=(f"XS{fc}_{tc}",), tag=f"res1{tc}_{fc}",
            )
            P.op(
                "ACT",
                lambda e, x=xc: e.activation(B.sq2[:], x, AF.Square),
                reads=(f"XS{fc}_{tc}",), writes=("sq2",), tag=f"rsq1{tc}_{fc}",
            )
            emit_stats_mms(P, B, f"XS{fc}_{tc}", xc, "sq2", 0, 1, fc, f"l1a{tc}_{fc}")
        ln_apply(
            P, B, tc, 0, 1,
            lambda c, tc=tc: f"XS{c}_{tc}",
            lambda c, tc=tc: xs_chunk(B, c, tc),
            lambda c, tc=tc: f"XS{c}_{tc}",
            lambda c, tc=tc: xs_chunk(B, c, tc),
            1, f"l1a{tc}",
        )




def build_l0_ffn(P, B, es, dram):
    nc = P.nc
    HB = es.enter_context(nc.sbuf_tensor("HB", [128, NC * T], BF16))
    Y1 = es.enter_context(nc.sbuf_tensor("Y1", [128, NC * 512], BF16))
    Z = es.enter_context(nc.sbuf_tensor("Z", [128, NC * 512], BF16))
    W = [es.enter_context(nc.sbuf_tensor(f"WF{i}", [128, NC * D], BF16)) for i in range(2)]
    LA = [es.enter_context(nc.sbuf_tensor(f"LAF{i}", [128, NC * 32], BF16)) for i in range(2)]
    LB = [es.enter_context(nc.sbuf_tensor(f"LBF{i}", [128, D], BF16)) for i in range(2)]
    U1 = es.enter_context(nc.sbuf_tensor("U1", [32, 512], BF16))
    BIASF = [es.enter_context(nc.sbuf_tensor(f"BIASF{i}", [128, D], BF16)) for i in range(2)]
    GB = es.enter_context(nc.sbuf_tensor("GBF", [128, 48], F32))
    QTF = [es.enter_context(nc.sbuf_tensor(f"QTF{i}", [128, 512], I32)) for i in range(1)]
    SCT = [es.enter_context(nc.sbuf_tensor(f"SCT{i}", [128, 512], F32)) for i in range(1)]

    P.dma("DMA_SP", GB[:], dram["l0f_gb"], writes=("GB",), tag="gb")

    def dequant(lin, tag):
        ws = lin % 2
        wres = f"WF{ws}"
        for kc in range(NC):
            for oh in range(2):
                sl = 0
                qt = QTF[sl]
                st = SCT[sl]
                P.dma(
                    "DMA_G",
                    qt[:],
                    dram["l0f_qT"].rearrange("l (c p) o -> l c p o", p=128)[
                        lin, kc, :, oh * 512 : oh * 512 + 512
                    ],
                    reads=(), writes=(f"QTF{sl}",), tag=f"qT{tag}_{kc}_{oh}",
                )
                P.dma(
                    "DMA_SP",
                    st[:],
                    dram["l0f_sT"].rearrange("l (c p) o -> l c p o", p=128)[
                        lin, kc, :, oh * 512 : oh * 512 + 512
                    ],
                    reads=(), writes=(f"SCT{sl}",), tag=f"sT{tag}_{kc}_{oh}",
                )
                scrq = scrslot(B, 0)
                P.op(
                    "DVE",
                    lambda e, qt=qt: e.tensor_scalar(scrq, qt[:], 1.0 / 7.5, -1.0, ALU.mult, ALU.add),
                    reads=(f"QTF{sl}",), writes=("scr0",), tag=f"dqa{tag}_{kc}_{oh}",
                )
                P.op(
                    "DVE",
                    lambda e, kc=kc, oh=oh, ws=ws, st=st: e.tensor_mul(
                        W[ws][:, kc * D + oh * 512 : kc * D + oh * 512 + 512], scrq, st[:]
                    ),
                    reads=("scr0", f"SCT{sl}"), writes=(wres,), tag=f"dqm{tag}_{kc}_{oh}",
                )
        P.dma(
            "DMA_G",
            LA[ws][:].rearrange("p (c r) -> p c r", r=32),
            dram["l0f_laT"].rearrange("l (c p) r -> l p c r", p=128)[lin],
            writes=(f"LAF{ws}",), tag=f"la{tag}",
        )
        P.dma(
            "DMA_G", LB[ws][:], dram["l0f_lbT_pad"][lin], writes=(f"LBF{ws}",), tag=f"lb{tag}"
        )
        P.dma(
            "DMA_G", BIASF[ws][:], dram["l0f_b_pad"][lin],
            writes=(f"BIASF{ws}",), tag=f"bf{tag}",
        )

    def lora_u1(ws, src_ap, src_res, tc, tag):
        for kc in range(NC):
            lhsT = LA[ws][:, kc * 32 : kc * 32 + 32]
            sap = src_ap(kc)
            P.op("PE", lambda e, l=lhsT: e.nop() if True else e.ldweights(l), reads=(f"LAF{ws}",), tag=f"ldla{tag}_{kc}")
            P.op(
                "PE",
                lambda e, l=lhsT, s=sap, kc=kc: e.matmul(
                    B.pb[3][0:32, :], l, s, start=(kc == 0), stop=(kc == NC - 1)
                ),
                reads=(f"LAF{ws}", src_res(kc), "pb3"), writes=("pb3",), tag=f"mmla{tag}_{kc}",
            )
        P.op(
            "ACT",
            lambda e: e.activation(U1[:], B.pb[3][0:32, :], AF.Copy),
            reads=("pb3",), writes=("U1",), tag=f"evu1{tag}",
        )

    def linear_group(ws, oc, src_ap, src_res, bank, tag):
        pres = f"pb{bank}"
        pp = B.pb[bank]
        for kc in range(NC):
            lhsT = W[ws][:, kc * D + oc * 128 : kc * D + oc * 128 + 128]
            sap = src_ap(kc)
            P.op("PE", lambda e, l=lhsT: e.nop() if True else e.ldweights(l), reads=(f"WF{ws}",), tag=f"ldw{tag}_{kc}")
            P.op(
                "PE",
                lambda e, l=lhsT, s=sap, kc=kc, p=pp: e.matmul(
                    p[:, :], l, s, start=(kc == 0), stop=False
                ),
                reads=(f"WF{ws}", src_res(kc), pres), writes=(pres,), tag=f"mm{tag}_{kc}",
            )
        lbt = LB[ws][0:32, oc * 128 : oc * 128 + 128]
        P.op("PE", lambda e, l=lbt: e.nop() if True else e.ldweights(l), reads=(f"LBF{ws}",), tag=f"ldlb{tag}")
        P.op(
            "PE",
            lambda e, l=lbt, p=pp: e.matmul(p[:, :], l, U1[:], start=False, stop=False),
            reads=(f"LBF{ws}", "U1", pres), writes=(pres,), tag=f"mmlb{tag}",
        )
        brow = BIASF[ws][0:1, oc * 128 : oc * 128 + 128]
        P.op("PE", lambda e, b=brow: e.nop() if True else e.ldweights(b), reads=(f"BIASF{ws}",), tag=f"ldbb{tag}")
        P.op(
            "PE",
            lambda e, b=brow, p=pp: e.matmul(p[:, :], b, B.ones_b[:], start=False, stop=True),
            reads=(f"BIASF{ws}", "ones_b", pres), writes=(pres,), tag=f"mmbb{tag}",
        )
        return pp, pres

    dequant(0, "d0")
    for sb in range(3):
        linA, linB = 2 * sb, 2 * sb + 1
        wsA, wsB = linA % 2, linB % 2
        if sb == 0:
            hs_ap = lambda kc, tc: xs_chunk(B, kc, tc)
            hs_res = lambda kc, tc: f"XS{kc}_{tc}"
        else:
            hs_ap = lambda kc, tc: HB[:, kc * T + tc * 512 : kc * T + tc * 512 + 512]
            hs_res = lambda kc, tc: f"HB{kc}_{tc}"
        dequant(linB, f"d{linB}")
        for tc in range(NTC):
            lora_u1(wsA, lambda kc, tc=tc: hs_ap(kc, tc), lambda kc, tc=tc: hs_res(kc, tc), tc, f"A{sb}_{tc}")
            for oc in range(NC):
                bank = 5 if oc % 2 == 0 else 2
                pp, pres = linear_group(
                    wsA, oc,
                    lambda kc, tc=tc: hs_ap(kc, tc),
                    lambda kc, tc=tc: hs_res(kc, tc),
                    bank, f"y1_{sb}_{tc}_{oc}",
                )
                ydst = Y1[:, oc * 512 : oc * 512 + 512]
                P.op(
                    "ACT",
                    lambda e, d=ydst, p=pp: e.activation(d, p[:, :], AF.Copy),
                    reads=(pres,), writes=(f"Y1_{oc}",), tag=f"evy1_{sb}_{tc}_{oc}",
                )
                P.op(
                    "ACT",
                    lambda e, p=pp: e.activation(B.sq2[:], p[:, :], AF.Square),
                    reads=(pres,), writes=("sq2",), tag=f"y1sq_{sb}_{tc}_{oc}",
                )
                emit_stats_mms(P, B, f"Y1_{oc}", ydst, "sq2", 0, 1, oc, f"f{sb}_{tc}_{oc}")
            mu = rowslot(B, 0)
            var = rowslot(B, 1)
            arow = rowslot(B, 2)
            brow2 = rowslot(B, 3)
            P.op("DVE", lambda e: e.tensor_scalar(mu, B.pb[0][0:1, :], 1.0 / D, None, ALU.mult),
                 reads=("pb0",), writes=("row0",), tag=f"fmu{sb}_{tc}")
            P.op("DVE", lambda e: e.tensor_scalar(var, B.pb[1][0:1, :], 1.0 / D, None, ALU.mult),
                 reads=("pb1",), writes=("row1",), tag=f"fmsq{sb}_{tc}")
            P.op("ACT", lambda e: e.activation(arow, mu, AF.Square), reads=("row0",), writes=("row2",), tag=f"fmusq{sb}_{tc}")
            P.op("DVE", lambda e: e.tensor_sub(var, var, arow), reads=("row1", "row2"), writes=("row1",), tag=f"fvar{sb}_{tc}")
            P.op("ACT", lambda e: e.activation(var, var, AF.Ln, bias=B.cpool[0:1, 0:1]),
                 reads=("row1", "cpool"), writes=("row1",), tag=f"fln{sb}_{tc}")
            P.op("ACT", lambda e: e.activation(arow, var, AF.Exp, scale=-0.5),
                 reads=("row1",), writes=("row2",), tag=f"frstd{sb}_{tc}")
            P.op("DVE", lambda e: e.tensor_mul(brow2, mu, arow), reads=("row0", "row2"), writes=("row3",), tag=f"fbrow{sb}_{tc}")
            P.op("PE", lambda e: e.matmul(B.pb[0][:, :], B.ones_f[0:1, 0:128], arow, start=True, stop=True),
                 reads=("row2", "ones_f", "pb0"), writes=("pb0",), tag=f"fbcA{sb}_{tc}")
            P.op("PE", lambda e: e.matmul(B.pb[1][:, :], B.ones_f[0:1, 0:128], brow2, start=True, stop=True),
                 reads=("row3", "ones_f", "pb1"), writes=("pb1",), tag=f"fbcB{sb}_{tc}")
            for oc in range(NC):
                ysrc = Y1[:, oc * 512 : oc * 512 + 512]
                scr0 = scrslot(B, 0)
                scr1 = scrslot(B, 1)
                P.op(
                    "DVE",
                    lambda e, y=ysrc: e.tensor_mul(scr0, y, B.pb[0][:, :]),
                    reads=(f"Y1_{oc}", "pb0"), writes=("scr0",), tag=f"fap1_{sb}_{tc}_{oc}",
                )
                P.op(
                    "DVE",
                    lambda e: e.tensor_sub(scr1, scr0, B.pb[1][:, :]),
                    reads=("scr0", "pb1"), writes=("scr0",), tag=f"fap2_{sb}_{tc}_{oc}",
                )
                gcol = GB[:, sb * 16 + oc : sb * 16 + oc + 1]
                bcol = GB[:, sb * 16 + 8 + oc : sb * 16 + 8 + oc + 1]
                zdst = Z[:, oc * 512 : oc * 512 + 512]
                P.op(
                    "ACT",
                    lambda e, z=zdst, g=gcol, bb=bcol: e.activation(
                        z, scr1, AF.Gelu, scale=g, bias=bb
                    ),
                    reads=("scr0", "GB"), writes=(f"Z_{oc}",), tag=f"fgelu{sb}_{tc}_{oc}",
                )
            lora_u1(
                wsB,
                lambda kc: Z[:, kc * 512 : kc * 512 + 512],
                lambda kc: f"Z_{kc}",
                tc, f"B{sb}_{tc}",
            )
            for oc in range(NC):
                bank = 5 if oc % 2 == 0 else 2
                pp, pres = linear_group(
                    wsB, oc,
                    lambda kc: Z[:, kc * 512 : kc * 512 + 512],
                    lambda kc: f"Z_{kc}",
                    bank, f"y2_{sb}_{tc}_{oc}",
                )
                hdst = HB[:, oc * T + tc * 512 : oc * T + tc * 512 + 512]
                if sb == 0:
                    P.op(
                        "DVE",
                        lambda e, h=hdst, p=pp, oc=oc, tc=tc: e.tensor_add(
                            h, p[:, :], xs_chunk(B, oc, tc)
                        ),
                        reads=(pres, f"XS{oc}_{tc}"), writes=(f"HB{oc}_{tc}",),
                        tag=f"hres{sb}_{tc}_{oc}",
                    )
                else:
                    P.op(
                        "DVE",
                        lambda e, h=hdst, p=pp: e.tensor_add(h, p[:, :], h),
                        reads=(pres, f"HB{oc}_{tc}"), writes=(f"HB{oc}_{tc}",),
                        tag=f"hres{sb}_{tc}_{oc}",
                    )
        if sb < 2:
            dequant(linA + 2, f"d{linA + 2}")

    for tc in range(NTC):
        for fc in range(NC):
            xc = xs_chunk(B, fc, tc)
            hc = HB[:, fc * T + tc * 512 : fc * T + tc * 512 + 512]
            P.op(
                "DVE",
                lambda e, x=xc, h=hc: e.tensor_add(x, h, x),
                reads=(f"HB{fc}_{tc}", f"XS{fc}_{tc}"), writes=(f"XS{fc}_{tc}",),
                tag=f"fres{tc}_{fc}",
            )
            P.op(
                "ACT",
                lambda e, x=xc: e.activation(B.sq2[:], x, AF.Square),
                reads=(f"XS{fc}_{tc}",), writes=("sq2",), tag=f"frsq{tc}_{fc}",
            )
            emit_stats_mms(P, B, f"XS{fc}_{tc}", xc, "sq2", 0, 1, fc, f"fo{tc}_{fc}")
        ln_apply(
            P, B, tc, 0, 1,
            lambda c, tc=tc: f"XS{c}_{tc}",
            lambda c, tc=tc: xs_chunk(B, c, tc),
            lambda c, tc=tc: f"XS{c}_{tc}",
            lambda c, tc=tc: xs_chunk(B, c, tc),
            1, f"fo{tc}",
        )




def build_l1_ffn(P, B, es, dram, xout):
    nc = P.nc
    R1 = es.enter_context(nc.sbuf_tensor("R1", [128, 32 * 512], BF16))
    R2 = es.enter_context(nc.sbuf_tensor("R2", [128, 32 * 512], BF16))
    NX = es.enter_context(nc.sbuf_tensor("NX", [128, NC * 512], BF16))
    K0S = [es.enter_context(nc.sbuf_tensor(f"K0S{i}", [128, NC * 128], BF16)) for i in range(2)]
    K1S = [es.enter_context(nc.sbuf_tensor(f"K1S{i}", [128, 32 * 128], BF16)) for i in range(2)]
    K2S = [es.enter_context(nc.sbuf_tensor(f"K2S{i}", [128, 32 * 128], BF16)) for i in range(2)]
    K1U = [es.enter_context(nc.sbuf_tensor(f"K1U{i}", [128, 32 * 128], U8)) for i in range(2)]
    RS = es.enter_context(nc.sbuf_tensor("RS", [1, 1024], BF16))
    OS = es.enter_context(nc.sbuf_tensor("OS", [128, 512], F32))
    SCC = es.enter_context(nc.sbuf_tensor("SCC", [128, 3], F32))

    P.dma("DMA_SP", SCC[:], dram["l1f_scol"], writes=("SCCr",), tag="scol")
    P.op("DVE", lambda e: e.tensor_scalar(SCC[:, 0:1], SCC[:, 0:1], 2.0, None, ALU.mult),
         reads=("SCCr",), writes=("SCCr",), tag="s0x2")
    P.op("DVE", lambda e: e.tensor_scalar(SCC[:, 1:2], SCC[:, 1:2], 4.0, None, ALU.mult),
         reads=("SCCr",), writes=("SCCr",), tag="s1x4")
    P.op("DVE", lambda e: e.tensor_scalar(SCC[:, 2:3], SCC[:, 2:3], 4.0, None, ALU.mult),
         reads=("SCCr",), writes=("SCCr",), tag="s2x4")

    k0t_r = dram["l1f_k0t"]
    k0b_r = dram["l1f_k0b"]
    k1_r = dram["l1f_k1t"]
    k2_r = dram["l1f_k2t"]
    xo = xout.rearrange("(c p) t -> c p t", p=128)

    def rowsum_neghalf(src_ap, src_res, nchunks, tag):
        for c in range(nchunks):
            P.op("PE", lambda e: e.nop() if True else e.ldweights(B.onescol[:, 0:1]), reads=("onescol",), tag=f"ldrs{tag}_{c}")
            P.op(
                "PE",
                lambda e, c=c: e.matmul(
                    B.pb[3][0:1, :], B.onescol[:, 0:1], src_ap(c),
                    start=(c == 0), stop=(c == nchunks - 1),
                ),
                reads=(src_res(c), "onescol", "pb3"), writes=("pb3",), tag=f"mmrs{tag}_{c}",
            )
        rsf = rowslot(B, 3)
        P.op(
            "ACT",
            lambda e: e.activation(rsf, B.pb[3][0:1, :], AF.Copy, scale=-0.5),
            reads=("pb3",), writes=("row3",), tag=f"evrs{tag}",
        )
        P.op(
            "DVE",
            lambda e: e.tensor_copy(RS[0:1, 0:512], rsf),
            reads=("row3",), writes=("RS",), tag=f"rshi{tag}",
        )
        P.op(
            "DVE",
            lambda e: e.tensor_sub(RS[0:1, 512:1024], rsf, RS[0:1, 0:512]),
            reads=("row3", "RS"), writes=("RS",), tag=f"rslo{tag}",
        )

    for tc in range(NTC):
        for kc in range(NC):
            P.op(
                "DVE",
                lambda e, kc=kc, tc=tc: e.tensor_scalar(
                    NX[:, kc * 512 : kc * 512 + 512], xs_chunk(B, kc, tc), -1.0, None, ALU.mult
                ),
                reads=(f"XS{kc}_{tc}",), writes=(f"NX{kc}",), tag=f"nx{tc}_{kc}",
            )
        for ft in range(32):
            sl = ft % 2
            bank = 5 if ft % 2 == 0 else 2
            pp = B.pb[bank]
            pres = f"pb{bank}"
            usl = ft % 2
            P.dma("DMA_SP", K1U[usl][:, 0 : NC * 128].rearrange("p (c f) -> p c f", f=128), k0t_r[ft],
                  writes=(f"K1Ua{usl}",), tag=f"k0t{tc}_{ft}")
            P.dma("DMA_SP", K1U[usl][:, NC * 128 : 2 * NC * 128].rearrange("p (c f) -> p c f", f=128), k0b_r[ft],
                  writes=(f"K1Ub{usl}",), tag=f"k0b{tc}_{ft}")
            P.op(
                "DVE" if ft % 2 == 0 else "POOL",
                lambda e, usl=usl, sl=sl: e.tensor_copy(K0S[sl][:], K1U[usl][:, 0 : NC * 128]),
                reads=(f"K1Ua{usl}",), writes=(f"K0Sa{sl}",), tag=f"cva{tc}_{ft}",
            )
            P.op(
                "POOL" if ft % 2 == 0 else "DVE",
                lambda e, usl=usl, sl=sl: e.tensor_copy(
                    K1S[sl][:, 0 : NC * 128], K1U[usl][:, NC * 128 : 2 * NC * 128]
                ),
                reads=(f"K1Ub{usl}",), writes=(f"K0Sb{sl}",), tag=f"cvb{tc}_{ft}",
            )
            for kc in range(NC):
                lhsT = K0S[sl][:, kc * 128 : kc * 128 + 128]
                P.op("PE", lambda e, l=lhsT: e.nop() if True else e.ldweights(l), reads=(f"K0Sa{sl}",), tag=f"ld0t{tc}_{ft}_{kc}")
                P.op(
                    "PE",
                    lambda e, l=lhsT, kc=kc, tc=tc, p=pp: e.matmul(
                        p[:, :], l, xs_chunk(B, kc, tc), start=(kc == 0), stop=False
                    ),
                    reads=(f"K0Sa{sl}", f"XS{kc}_{tc}", pres), writes=(pres,), tag=f"mm0t{tc}_{ft}_{kc}",
                )
            for kc in range(NC):
                lhsT = K1S[sl][:, kc * 128 : kc * 128 + 128]
                P.op("PE", lambda e, l=lhsT: e.nop() if True else e.ldweights(l), reads=(f"K0Sb{sl}",), tag=f"ld0b{tc}_{ft}_{kc}")
                P.op(
                    "PE",
                    lambda e, l=lhsT, kc=kc, p=pp: e.matmul(
                        p[:, :], l, NX[:, kc * 512 : kc * 512 + 512],
                        start=False, stop=(kc == NC - 1),
                    ),
                    reads=(f"K0Sb{sl}", f"NX{kc}", pres), writes=(pres,), tag=f"mm0b{tc}_{ft}_{kc}",
                )
            P.op(
                "ACT",
                lambda e, ft=ft, p=pp: e.activation(
                    R1[:, ft * 512 : ft * 512 + 512], p[:, :], AF.Relu, scale=SCC[:, 0:1]
                ),
                reads=(pres, "SCCr"), writes=(f"R1_{ft}",), tag=f"ev1{tc}_{ft}",
            )
        rowsum_neghalf(
            lambda c: R1[:, c * 512 : c * 512 + 512], lambda c: f"R1_{c}", 32, f"rs1_{tc}"
        )
        for ft in range(32):
            sl = ft % 2
            bank = 5 if ft % 2 == 0 else 2
            pp = B.pb[bank]
            pres = f"pb{bank}"
            usl = ft % 2
            P.dma("DMA_SP", K1U[usl][:].rearrange("p (c f) -> p c f", f=128), k1_r[ft],
                  writes=(f"K1Ua{usl}", f"K1Ub{usl}"), tag=f"k1{tc}_{ft}")
            for qq in range(2):
                P.op(
                    ("DVE", "POOL")[(ft + qq) % 2],
                    lambda e, usl=usl, sl=sl, qq=qq: e.tensor_copy(
                        K1S[sl][:, qq * 2048 : qq * 2048 + 2048],
                        K1U[usl][:, qq * 2048 : qq * 2048 + 2048],
                    ),
                    reads=(f"K1Ua{usl}",), writes=(f"K1S{sl}", f"K0Sb{sl}"), tag=f"cv1{tc}_{ft}_{qq}",
                )
            for kc in range(32):
                lhsT = K1S[sl][:, kc * 128 : kc * 128 + 128]
                P.op("PE", lambda e, l=lhsT: e.nop() if True else e.ldweights(l), reads=(f"K1S{sl}",), tag=f"ld1{tc}_{ft}_{kc}")
                P.op(
                    "PE",
                    lambda e, l=lhsT, kc=kc, p=pp: e.matmul(
                        p[:, :], l, R1[:, kc * 512 : kc * 512 + 512],
                        start=(kc == 0), stop=False,
                    ),
                    reads=(f"K1S{sl}", f"R1_{kc}", pres), writes=(pres,), tag=f"mm1{tc}_{ft}_{kc}",
                )
            P.op("PE", lambda e: e.nop() if True else e.ldweights(B.ones_b[0:1, 0:128]), reads=("ones_b",), tag=f"ldr1{tc}_{ft}")
            P.op(
                "PE",
                lambda e, p=pp: e.matmul(p[:, :], B.ones_b[0:1, 0:128], RS[0:1, 0:512], start=False, stop=False),
                reads=("ones_b", "RS", pres), writes=(pres,), tag=f"mmr1{tc}_{ft}",
            )
            P.op(
                "PE",
                lambda e, p=pp: e.matmul(p[:, :], B.ones_b[0:1, 0:128], RS[0:1, 512:1024], start=False, stop=True),
                reads=("ones_b", "RS", pres), writes=(pres,), tag=f"mmr1b{tc}_{ft}",
            )
            P.op(
                "ACT",
                lambda e, ft=ft, p=pp: e.activation(
                    R2[:, ft * 512 : ft * 512 + 512], p[:, :], AF.Relu, scale=SCC[:, 1:2]
                ),
                reads=(pres, "SCCr"), writes=(f"R2_{ft}",), tag=f"ev2{tc}_{ft}",
            )
        rowsum_neghalf(
            lambda c: R2[:, c * 512 : c * 512 + 512], lambda c: f"R2_{c}", 32, f"rs2_{tc}"
        )
        for oc in range(NC):
            sl = oc % 2
            bank = 5 if oc % 2 == 0 else 2
            pp = B.pb[bank]
            pres = f"pb{bank}"
            usl = oc % 2
            P.dma("DMA_SP", K1U[usl][:].rearrange("p (c f) -> p c f", f=128), k2_r[oc],
                  writes=(f"K1Ua{usl}", f"K1Ub{usl}"), tag=f"k2{tc}_{oc}")
            for qq in range(2):
                P.op(
                    ("DVE", "POOL")[(oc + qq) % 2],
                    lambda e, usl=usl, sl=sl, qq=qq: e.tensor_copy(
                        K2S[sl][:, qq * 2048 : qq * 2048 + 2048],
                        K1U[usl][:, qq * 2048 : qq * 2048 + 2048],
                    ),
                    reads=(f"K1Ua{usl}",), writes=(f"K2S{sl}",), tag=f"cv2{tc}_{oc}_{qq}",
                )
            for kc in range(32):
                lhsT = K2S[sl][:, kc * 128 : kc * 128 + 128]
                P.op("PE", lambda e, l=lhsT: e.nop() if True else e.ldweights(l), reads=(f"K2S{sl}",), tag=f"ld2{tc}_{oc}_{kc}")
                P.op(
                    "PE",
                    lambda e, l=lhsT, kc=kc, p=pp: e.matmul(
                        p[:, :], l, R2[:, kc * 512 : kc * 512 + 512],
                        start=(kc == 0), stop=False,
                    ),
                    reads=(f"K2S{sl}", f"R2_{kc}", pres), writes=(pres,), tag=f"mm2{tc}_{oc}_{kc}",
                )
            P.op("PE", lambda e: e.nop() if True else e.ldweights(B.ones_b[0:1, 0:128]), reads=("ones_b",), tag=f"ldr2{tc}_{oc}")
            P.op(
                "PE",
                lambda e, p=pp: e.matmul(p[:, :], B.ones_b[0:1, 0:128], RS[0:1, 0:512], start=False, stop=False),
                reads=("ones_b", "RS", pres), writes=(pres,), tag=f"mmr2{tc}_{oc}",
            )
            P.op(
                "PE",
                lambda e, p=pp: e.matmul(p[:, :], B.ones_b[0:1, 0:128], RS[0:1, 512:1024], start=False, stop=True),
                reads=("ones_b", "RS", pres), writes=(pres,), tag=f"mmr2b{tc}_{oc}",
            )
            scr = scrslot(B, 0)
            P.op(
                "ACT",
                lambda e, p=pp: e.activation(scr, p[:, :], AF.Copy, scale=SCC[:, 2:3]),
                reads=(pres, "SCCr"), writes=("scr0",), tag=f"ev3{tc}_{oc}",
            )
            xc = xs_chunk(B, oc, tc)
            P.op(
                "DVE",
                lambda e, x=xc: e.tensor_add(x, scr, x),
                reads=("scr0", f"XS{oc}_{tc}"), writes=(f"XS{oc}_{tc}",), tag=f"res2{tc}_{oc}",
            )
            P.op(
                "ACT",
                lambda e, x=xc: e.activation(B.sq2[:], x, AF.Square),
                reads=(f"XS{oc}_{tc}",), writes=("sq2",), tag=f"rsq2{tc}_{oc}",
            )
            emit_stats_mms(P, B, f"XS{oc}_{tc}", xc, "sq2", 0, 1, oc, f"l1f{tc}_{oc}")
        mu = rowslot(B, 0)
        var = rowslot(B, 1)
        arow = rowslot(B, 2)
        brow2 = rowslot(B, 3)
        P.op("DVE", lambda e: e.tensor_scalar(mu, B.pb[0][0:1, :], 1.0 / D, None, ALU.mult),
             reads=("pb0",), writes=("row0",), tag=f"zmu{tc}")
        P.op("DVE", lambda e: e.tensor_scalar(var, B.pb[1][0:1, :], 1.0 / D, None, ALU.mult),
             reads=("pb1",), writes=("row1",), tag=f"zmsq{tc}")
        P.op("ACT", lambda e: e.activation(arow, mu, AF.Square), reads=("row0",), writes=("row2",), tag=f"zmusq{tc}")
        P.op("DVE", lambda e: e.tensor_sub(var, var, arow), reads=("row1", "row2"), writes=("row1",), tag=f"zvar{tc}")
        P.op("ACT", lambda e: e.activation(var, var, AF.Ln, bias=B.cpool[0:1, 0:1]),
             reads=("row1", "cpool"), writes=("row1",), tag=f"zln{tc}")
        P.op("ACT", lambda e: e.activation(arow, var, AF.Exp, scale=-0.5),
             reads=("row1",), writes=("row2",), tag=f"zrstd{tc}")
        P.op("DVE", lambda e: e.tensor_mul(brow2, mu, arow), reads=("row0", "row2"), writes=("row3",), tag=f"zbrow{tc}")
        P.op("PE", lambda e: e.matmul(B.pb[3][:, :], B.ones_f[0:1, 0:128], arow, start=True, stop=True),
             reads=("row2", "ones_f", "pb3"), writes=("pb3",), tag=f"zbcA{tc}")
        P.op("PE", lambda e: e.matmul(B.pb[4][:, :], B.ones_f[0:1, 0:128], brow2, start=True, stop=True),
             reads=("row3", "ones_f", "pb4"), writes=("pb4",), tag=f"zbcB{tc}")
        scr = scrslot(B, 0)
        for c in range(NC):
            xc = xs_chunk(B, c, tc)
            P.op(
                "DVE",
                lambda e, x=xc: e.tensor_mul(scr, x, B.pb[3][:, :]),
                reads=(f"XS{c}_{tc}", "pb3"), writes=("scr0",), tag=f"zap1{tc}_{c}",
            )
            P.op(
                "DVE",
                lambda e: e.tensor_sub(OS[:], scr, B.pb[4][:, :]),
                reads=("scr0", "pb4"), writes=("OS",), tag=f"zap2{tc}_{c}",
            )
            P.dma(
                "DMA_SP",
                xo[c, :, tc * 512 : tc * 512 + 512],
                OS[:],
                reads=("OS",), writes=(f"xout{c}_{tc}",), tag=f"zout{c}_{tc}",
            )




def dtype_of(v):
    return {
        np.dtype(np.float32): F32,
        np.dtype(np.int32): I32,
        np.dtype(np.uint8): U8,
        np.dtype(np.bool_): U8,
    }[v.dtype]


def build_program(dram_specs, phases, debug_out=None):
    nc = bass.Bass("TRN2", target_bir_lowering=False)
    dram = {}
    for name, (shape, dt) in dram_specs.items():
        dram[name] = nc.dram_tensor(name, list(shape), dt, kind="ExternalInput").ap()
    xout = nc.dram_tensor("xout", [D, T], F32, kind="ExternalOutput").ap()
    global DBG_R
    if DBG:
        DBG_R = [nc.dram_tensor("rdbg", [D, T], BF16, kind="ExternalOutput").ap()]
    else:
        DBG_R = None

    with ExitStack() as es:
        P = Prog(nc)
        B = build_common(nc, es)
        sems = {}
        for e in ENGINES + DMA_QUEUES:
            sems[e] = es.enter_context(nc.semaphore(f"sem_{e}"))
        block = es.enter_context(nc.Block())

        load_common(P, B, dram)
        if "l0a" in phases:
            with ExitStack() as esp:
                build_l0_attn(P, B, esp, dram)
                P.barrier("bar_l0a")
                P.finalize_phase(block, sems)
        if "l0f" in phases:
            with ExitStack() as esp:
                build_l0_ffn(P, B, esp, dram)
                P.barrier("bar_l0f")
                P.finalize_phase(block, sems)
        if "l1a" in phases:
            with ExitStack() as esp:
                build_l1_attn(P, B, esp, dram)
                P.barrier("bar_l1a")
                P.finalize_phase(block, sems)
        if "l1f" in phases:
            with ExitStack() as esp:
                build_l1_ffn(P, B, esp, dram, xout)
                P.barrier("bar_l1f")
                P.finalize_phase(block, sems)
        xo = xout.rearrange("(c p) t -> c p t", p=128)
        if "l1f" in phases:
            P.barrier("bar_out")
            P.finalize_phase(block, sems)
            print("PROG:", P.stats())
            return nc, P
        for c in range(NC):
            for tc in range(NTC):
                slot = tc % 2
                scr = scrslot(B, slot)
                P.op(
                    "DVE",
                    lambda e, c=c, tc=tc, scr=scr: e.tensor_copy(scr, xs_chunk(B, c, tc)),
                    reads=(f"XS{c}_{tc}",), writes=("scr0",), tag=f"osc{c}_{tc}",
                )
                P.dma(
                    "DMA_SP",
                    xo[c, :, tc * 512 : tc * 512 + 512],
                    scr,
                    reads=("scr0",), writes=(f"xout{c}_{tc}",), tag=f"out{c}_{tc}",
                )
        P.barrier("bar_out")
        P.finalize_phase(block, sems)
    return nc, P


def kernel(**inputs):
    cores = [host_layout(core, inputs) for core in range(8)]
    specs = {k: (v.shape, dtype_of(v)) for k, v in cores[0].items()}
    nc, P = build_program(specs, phases=("l0a", "l0f", "l1a", "l1f"))
    res = run_bass_kernel_spmd(nc, cores, core_ids=list(range(8)))
    out = np.stack([np.asarray(r["xout"]).T for r in res.results])
    return out.astype(np.float32)
